# version 13
# baseline (speedup 1.0000x reference)
import numpy as np
import concourse.bass as bass
import concourse.mybir as mybir
from concourse.bass_utils import run_bass_kernel_spmd

F32 = mybir.dt.float32
BF16 = mybir.dt.bfloat16
ALU = mybir.AluOpType
ACT = mybir.ActivationFunctionType
AXL = mybir.AxisListType

D = 1024
DK = 1024
DV = 2048
H = 4
HDK = DK // H
HDV = DV // H
NE = 4
LR = 16
NDC = D // 128
NKC = DK // 128
NVC = DV // 128
CH = 128
GN = 16.0
LN16 = float(np.log(1.0 / GN))
EPS = 1e-5


class Cfg:
    def __init__(self, ST=1024, NST=2, NPST=2, debug=False, stop_after=None):
        self.ST = ST
        self.NST = NST
        self.NPST = NPST
        self.T = ST * NST
        self.P = ST * NPST
        self.NTC = ST // 128
        self.NBLK = ST // 512
        self.debug = debug
        self.stop_after = stop_after


class Sem:
    def __init__(self, h, name):
        self.h = h
        self.n = 0
        self.name = name


class Prog:
    ENGINES = ("sync", "scalar", "vector", "tensor", "gpsimd")

    def __init__(self):
        self.q = {e: [] for e in self.ENGINES}

    def emit(self, eng, fn, wait=(), inc=None, inck=1):
        wait = tuple((s, v) for (s, v) in wait if v > 0)
        for s, v in wait:
            assert v <= s.n, f"wait on future {s.name}: {v} > {s.n}"
        if inc is not None:
            inc.n += inck

        def run(e, fn=fn, wait=wait, inc=inc, inck=inck):
            for s, v in wait:
                e.wait_ge(s.h, v)
            r = fn(e)
            if inc is not None:
                r.then_inc(inc.h, inck)
        self.q[eng].append(run)


def _rearr(ap):
    return ap.rearrange("(c p) n -> p c n", p=128)


MM_TAGS = []
MM_BIR = {}


def build(cfg: Cfg):
    MM_TAGS.clear()
    MM_BIR.clear()
    nc = bass.Bass("TRN2", target_bir_lowering=False,
                   dynamic_dma_scratch_size=16384)
    ST, NTC, NBLK = cfg.ST, cfg.NTC, cfg.NBLK
    NCH = ST // CH

    dt = nc.dram_tensor
    x_tm = dt("x_tm", [cfg.T, D], F32, kind="ExternalInput").ap()
    xp_tm = dt("xp_tm", [cfg.P, D], F32, kind="ExternalInput").ap()
    Wq = dt("Wq", [D, DK], F32, kind="ExternalInput").ap()
    Wk = dt("Wk", [D, DK], F32, kind="ExternalInput").ap()
    Wv = dt("Wv", [D, DV], F32, kind="ExternalInput").ap()
    Wg = dt("Wg", [D, DV], F32, kind="ExternalInput").ap()
    Wgk1 = dt("Wgk1", [128, NDC * LR], F32, kind="ExternalInput").ap()
    Wgk2 = dt("Wgk2", [LR, DK], F32, kind="ExternalInput").ap()
    bgk2c = dt("bgk2c", [128, NKC], F32, kind="ExternalInput").ap()
    Wk_p = dt("Wk_p", [D, DK], F32, kind="ExternalInput").ap()
    Wv_p = dt("Wv_p", [D, DV], F32, kind="ExternalInput").ap()
    Wgk1_p = dt("Wgk1_p", [128, NDC * LR], F32, kind="ExternalInput").ap()
    Wgk2_p = dt("Wgk2_p", [LR, DK], F32, kind="ExternalInput").ap()
    bgk2c_p = dt("bgk2c_p", [128, NKC], F32, kind="ExternalInput").ap()
    Wo = dt("Wo", [DV, D], F32, kind="ExternalInput").ap()
    rms_c = dt("rms_c", [128, 4], F32, kind="ExternalInput").ap()
    gateW = dt("gateW", [128, NDC * NE], F32, kind="ExternalInput").ap()
    gatebr = dt("gatebr", [1, NE], F32, kind="ExternalInput").ap()
    W1 = dt("W1", [NE, D, D], F32, kind="ExternalInput").ap()
    W2 = dt("W2", [NE, D, D], F32, kind="ExternalInput").ap()
    b1r = dt("b1r", [NE, D], F32, kind="ExternalInput").ap()
    b2r = dt("b2r", [NE, D], F32, kind="ExternalInput").ap()
    be2r = dt("be2r", [NE, D], F32, kind="ExternalInput").ap()
    g1b = dt("g1b", [NE, 128, D], F32, kind="ExternalInput").ap()
    be1b = dt("be1b", [NE, 128, D], F32, kind="ExternalInput").ap()
    g2b = dt("g2b", [NE, 128, D], F32, kind="ExternalInput").ap()
    identf = dt("identf", [128, 128], F32, kind="ExternalInput").ap()
    identb = dt("identb", [128, 128], BF16, kind="ExternalInput").ap()
    maskU = dt("maskU", [128, 128], F32, kind="ExternalInput").ap()
    onesrow = dt("onesrow", [1, 512], BF16, kind="ExternalInput").ap()
    onescan = dt("onescan", [128, 128], F32, kind="ExternalInput").ap()
    cvals = dt("cvals", [128, 3], F32, kind="ExternalInput").ap()

    out_d = dt("out", [cfg.T, D], F32, kind="ExternalOutput").ap()
    dbg = {}
    if cfg.debug:
        def dout(nm, shape, dtp):
            dbg[nm] = dt("dbg_" + nm, shape, dtp, kind="ExternalOutput").ap()
        dout("xn", [128, NDC, ST], BF16)
        dout("bb", [128, NKC, ST], F32)
        dout("eb", [128, NKC, NCH], F32)
        dout("w", [128, NTC * NE], F32)
        dout("qg", [128, NKC, ST], BF16)
        dout("kd", [128, NKC, ST], BF16)
        dout("kgT", [128, NCH, NKC, 128], BF16)
        dout("v", [128, NTC, DV], BF16)
        dout("silu", [128, NTC, DV], BF16)
        dout("S", [128, NKC, HDV], BF16)
        dout("attn", [128, NTC, D], BF16)
        dout("cols", [128, 8, NTC], F32)

    TOP = 224 * 1024 - 64
    arena = nc.alloc_sbuf_tensor("arena", [128, 207 * 1024], mybir.dt.uint8)
    BASE = nc.lookup_mloc(arena).addr
    cur = [BASE]

    offs = {}

    def alloc(name, shape, dtype, at=None):
        nbytes = int(np.prod(shape[1:])) * (2 if dtype == BF16 else 4)
        if at is None:
            a = (cur[0] + 31) // 32 * 32
            cur[0] = a + nbytes
        else:
            a = at
        assert a + nbytes <= TOP, f"SBUF overflow at {name}: {a + nbytes}"
        offs[name] = a
        return nc.alloc_sbuf_tensor_at(name, list(shape), dtype, offset=a)

    s_xn = alloc("s_xn", [128, NDC, ST], BF16)
    s_qg = alloc("s_qg", [128, NKC, ST], BF16)
    s_kd = alloc("s_kd", [128, NKC, ST], BF16)
    s_kgT = alloc("s_kgT", [128, NCH, NKC, 128], BF16)
    s_S = alloc("s_S", [128, NKC, HDV], BF16)

    attn_sz = NTC * D * 2
    big_sz = max(NKC * ST * 4,
                 attn_sz + DV * 2 + NVC * 128 * 2 + 2048 + 4096)
    big0 = (cur[0] + 31) // 32 * 32
    cur[0] = big0 + big_sz
    s_bb = alloc("s_bb", [128, NKC, ST], F32, at=big0)
    s_Eq = alloc("s_Eq", [128, NKC, ST], BF16, at=big0)
    s_Em = alloc("s_Em", [128, NKC, ST], BF16, at=offs["s_kgT"])
    s_attn = alloc("s_attn", [128, NTC, D], BF16, at=big0)
    s_ogc = alloc("s_ogc", [128, DV], BF16, at=big0 + attn_sz)
    s_ogf = alloc("s_ogf", [128, NVC, 128], BF16, at=big0 + attn_sz + DV * 2)
    s_kgs = alloc("s_kgs", [128, 1024], BF16, at=big0 + attn_sz)
    jb_off = big0 + attn_sz + DV * 2 + NVC * 128 * 2
    s_jb = alloc("s_jb", [128, 1024], BF16, at=jb_off)
    s_gk1 = alloc("s_gk1", [16, ST], BF16, at=jb_off)
    s_rb = alloc("s_rb", [1, 2048], BF16, at=big0 + attn_sz)
    s_be2 = alloc("s_be2", [4, D], BF16, at=big0 + attn_sz + DV * 2)
    s_AT = alloc("s_AT", [128, H, 128], BF16, at=jb_off + 1024)
    s_rb2 = alloc("s_rb2", [1, 1024], BF16,
                  at=big0 + attn_sz + DV * 2 + 2048)
    s_diag = alloc("s_diag", [128, NKC, 128], BF16, at=jb_off + 4096)
    s_wgk2 = alloc("s_wgk2", [16, NKC, 128], BF16, at=jb_off + 2048)

    wslot = [alloc(f"s_w{i}", [128, 4096], BF16) for i in range(4)]

    v0 = (cur[0] + 31) // 32 * 32
    cur[0] = v0 + max(2 * NKC * ST * 2, NTC * DV * 2, 2 * NTC * D * 2)
    s_v = alloc("s_v", [128, NTC, DV], BF16, at=v0)
    s_z = alloc("s_z", [128, NTC, D], BF16, at=v0)
    s_h = alloc("s_h", [128, NTC, D], BF16, at=v0 + NTC * D * 2)

    hfm_sz = NDC * ST * 2
    si_sz = max(NTC * DV * 2, hfm_sz + 2 * D * 2,
                16384 + 8192 + 4096 + 4096)
    si0 = (cur[0] + 31) // 32 * 32
    cur[0] = si0 + si_sz
    s_silu = alloc("s_silu", [128, NTC, DV], BF16, at=si0)
    s_sg = alloc("s_sg", [128, NKC, ST], BF16, at=si0)
    s_xt = alloc("s_xt", [128, 2, 4, D], BF16, at=offs["s_qg"])
    s_xntm = alloc("s_xntm", [128, 2, 4, D], BF16, at=offs["s_kd"])
    s_w4 = alloc("s_w4", [128, NDC, 512], BF16, at=offs["s_qg"])
    s_w5 = alloc("s_w5", [128, NDC, 512], BF16, at=offs["s_kd"])
    s_hfm = alloc("s_hfm", [128, NDC, ST], BF16, at=si0)
    s_bc1 = alloc("s_bc1", [128, D], BF16, at=si0 + hfm_sz)
    s_bc2 = alloc("s_bc2", [128, D], BF16, at=si0 + hfm_sz + D * 2)
    s_gx = alloc("s_gx", [128, 2, D], F32, at=si0 + 16384)
    s_cf = alloc("s_cf", [128, 1024], F32, at=si0 + 16384 + 8192)
    s_jf = alloc("s_jf", [128, 1024], F32, at=si0 + 16384 + 8192 + 4096)

    s_eb = alloc("s_eb", [128, 2, NKC, NCH], F32)
    s_cols = alloc("s_cols", [128, 2, 8, NTC], F32)
    s_mcol = alloc("s_mcol", [128, 2, 4, NTC], F32)
    s_gsc = alloc("s_gsc", [128, 96], F32)
    s_wc = alloc("s_wc", [128, NTC, NE], F32)
    s_wfm = alloc("s_wfm", [4, NTC, 128], BF16)
    s_scol = alloc("s_scol", [128, 2, NTC, 2], F32)
    s_ssq = alloc("s_ssq", [128, 3, NTC], F32)
    s_wgk1 = alloc("s_wgk1", [128, NDC, LR], BF16)
    s_gateW = alloc("s_gateW", [128, NDC, NE], BF16)
    s_rms = alloc("s_rms", [128, 4], F32)
    s_bgk = alloc("s_bgk", [128, NKC], F32)
    c_idf = alloc("c_idf", [128, 128], F32)
    c_idb = alloc("c_idb", [128, 128], BF16)
    c_mU = alloc("c_mU", [128, 128], F32)
    c_os = alloc("c_os", [128, 128], F32)
    c_or1 = alloc("c_or1", [1, 128], BF16)
    s_gateb = alloc("s_gateb", [1, NE], BF16)
    s_cv = alloc("s_cv", [128, 3], F32)
    assert cur[0] <= BASE + 207 * 1024, f"SBUF arena overflow: {cur[0] - BASE}"
    print("SBUF used:", cur[0] - BASE)

    pa = nc.alloc_psum_tensor("pa", [128, 512], F32)
    pb = nc.alloc_psum_tensor("pb", [128, 512], F32)
    pc_ = nc.alloc_psum_tensor("pc", [128, 512], F32)
    pd = nc.alloc_psum_tensor("pd", [128, 512], F32)
    pf1 = nc.alloc_psum_tensor("pf1", [128, 1024], F32)
    pf2 = nc.alloc_psum_tensor("pf2", [128, 1024], F32)

    sem_ctx = []
    P = Prog()

    def mksem(name):
        cm = nc.semaphore(name)
        h = cm.__enter__()
        sem_ctx.append(cm)
        return Sem(h, name)

    gd = mksem("gd")
    gx = mksem("gx")
    gbc = mksem("gbc")
    gbr = mksem("gbr")
    gsl = [mksem(f"gs{i}") for i in range(4)]
    tg = [mksem("tgA"), mksem("tgB")]
    gbr2 = [mksem("gbrA"), mksem("gbrB")]
    pl = mksem("pl")
    gs4 = mksem("gs4")
    gs5 = mksem("gs5")
    xd = mksem("xd")
    tp = mksem("tp")
    te = mksem("te")
    ve = mksem("ve")
    sc = mksem("sc")

    def slotv(i, c, n):
        return wslot[i][:, :].rearrange("p (c n) -> p c n", n=n)[:, 0:c, :]

    def g_op(fn, wait=()):
        P.emit("gpsimd", fn, wait=wait, inc=pl)

    def g_dma(dst_ap, src_ap, wait=(), sem=None):
        P.emit("gpsimd", lambda e, d=dst_ap, s=src_ap: e.dma_start(out=d, in_=s),
               wait=wait, inc=sem or gd, inck=16)

    def x_dma(dst_ap, src_ap, wait=()):
        P.emit("scalar", lambda e, d=dst_ap, s=src_ap: e.dma_start(out=d, in_=s),
               wait=wait, inc=xd, inck=16)

    def t_dma(dst_ap, src_ap, wait=()):
        P.emit("sync",
               lambda e, d=dst_ap, s=src_ap: e.dma_start(out=d, in_=s, transpose=True),
               wait=wait, inc=tp, inck=16)

    def mm(out, lhsT, rhs, start, stop, wait=(), inc=None):
        import sys as _sys
        tag = _sys._getframe(1).f_lineno
        wdesc = ",".join(f"{s.name}>={v}" for s, v in wait)

        def _do(e, o=out, l=lhsT, r=rhs, st=start, sp=stop, tag=tag, wd=wdesc):
            ret = e.matmul(out=o, lhsT=l, rhs=r, start=st, stop=sp)
            try:
                MM_BIR[ret.ins.name] = (tag, wd)
            except Exception:
                pass
            return ret
        P.emit("tensor", _do, wait=wait, inc=inc)

    def v_op(fn, wait=(), inc=None):
        P.emit("vector", fn, wait=tuple(wait) + ((ve, ve.n),), inc=inc or ve)

    def s_op(fn, wait=(), inc=None):
        P.emit("scalar", fn, wait=tuple(wait) + ((sc, sc.n),), inc=inc or sc)

    wu_i = [0]
    wu_freed = [None, None, None, None]

    def wu_next(dram_ap2d):
        i = wu_i[0] % 4
        wu_i[0] += 1
        r, ncol = dram_ap2d.shape
        w = [wu_freed[i]] if wu_freed[i] is not None else []
        g_dma(slotv(i, r // 128, ncol), _rearr(dram_ap2d), wait=w, sem=gsl[i])
        return i

    def wu_done(i):
        wu_freed[i] = (te, te.n)

    class Rot:
        def __init__(self, banks):
            self.banks = banks
            self.mark = [None] * len(banks)
            self.i = 0

        def next(self):
            b = self.i % len(self.banks)
            self.i += 1
            m = self.mark[b]
            return self.banks[b], ([m] if m is not None else [])

        def done(self, sem):
            b = (self.i - 1) % len(self.banks)
            self.mark[b] = (sem, sem.n)

    rot = Rot([pa, pb])
    rot2 = Rot([pc_, pd, pf2[:, 512:1024]])
    rot2p = Rot([pf1[:, 0:512], pf1[:, 512:1024], pf2[:, 0:512]])

    g_dma(c_idf[:, :], identf)
    g_dma(c_idb[:, :], identb)
    g_dma(c_mU[:, :], maskU)
    g_dma(c_os[:, :], onescan)
    g_dma(s_cv[:, :], cvals)
    g_dma(s_rms[:, :], rms_c)
    g_dma(c_or1[0:1, :], onesrow[0:1, 0:128])
    g_dma(s_gateb[:, :], gatebr)
    cm = gd.n
    v_op(lambda e: e.memset(s_S[:, :, :], 0.0), wait=[(gd, cm)])

    ones1 = c_or1[0:1, 0:128]

    def stage_A_blk(x_src, t0, blk, cpar, load_w, sq_w=()):
        par = blk % 2
        bsl = s_xt[:, par, :, :]
        xnb = s_xntm[:, par, :, :]
        rows = x_src[t0 + blk * 512: t0 + (blk + 1) * 512, :]
        g_dma(bsl[:, :, :], _rearr(rows), wait=[(gx, gx.n)] + list(load_w), sem=gx)
        gw = gx.n
        q = s_cols[:, cpar]
        for i in range(4):
            tc = blk * 4 + i
            s_op(lambda e, b=bsl, xn=xnb, i=i, tc=tc: e.activation(
                out=xn[:, i, :], in_=b[:, i, :], func=ACT.Square,
                accum_out=s_ssq[:, 2, tc:tc + 1]),
                wait=[(gx, gw), (gd, cm)] +
                     (([(tg[par], tg[par].n)] + list(sq_w)) if i == 0 else []))
            v_op(lambda e, b=bsl, q=q, i=i, tc=tc: e.reduce_sum(
                out=q[:, 0, tc:tc + 1], in_=b[:, i, :], axis=AXL.X),
                wait=[(gx, gw)])
        t = slice(blk * 4, blk * 4 + 4)
        v_op(lambda e, q=q, t=t: e.tensor_scalar(
            out=q[:, 1, t], in0=q[:, 0, t], scalar1=1.0 / D, scalar2=None,
            op0=ALU.mult), wait=[(ve, ve.n)])
        v_op(lambda e, q=q, t=t: e.tensor_tensor(
            out=q[:, 2, t], in0=q[:, 1, t], in1=q[:, 1, t], op=ALU.mult))
        v_op(lambda e, q=q, t=t: e.scalar_tensor_tensor(
            out=q[:, 3, t], in0=s_ssq[:, 2, t], scalar=1.0 / D, in1=q[:, 2, t],
            op0=ALU.mult, op1=ALU.subtract), wait=[(sc, sc.n)])
        s_op(lambda e, q=q, t=t: e.activation(out=q[:, 4, t], in_=q[:, 3, t],
             func=ACT.Ln, bias=s_cv[:, 0:1]), wait=[(ve, ve.n)])
        s_op(lambda e, q=q, t=t: e.activation(out=q[:, 5, t], in_=q[:, 4, t],
             func=ACT.Exp, scale=-0.5))
        v_op(lambda e, q=q, t=t: e.scalar_tensor_tensor(
            out=q[:, 6, t], in0=q[:, 1, t], scalar=-1.0, in1=q[:, 5, t],
            op0=ALU.mult, op1=ALU.mult), wait=[(sc, sc.n)])
        v_op(lambda e, q=q, t=t: e.tensor_scalar(
            out=q[:, 7, t], in0=q[:, 5, t], scalar1=1.0, scalar2=None,
            op0=ALU.add))
        for i in range(4):
            tc = blk * 4 + i
            v_op(lambda e, q=q, i=i, tc=tc, b=bsl, xn=xnb: e.tensor_scalar(
                out=xn[:, i, :], in0=b[:, i, :],
                scalar1=q[:, 5, tc:tc + 1], scalar2=q[:, 6, tc:tc + 1],
                op0=ALU.mult, op1=ALU.add),
                wait=[(ve, ve.n)])
        return (par, xnb, blk, (ve, ve.n))

    def stage_A_xpose(blks, xn_free):
        xnm = []
        first = [xn_free] if xn_free is not None else []
        for par, xnb, blk, am in blks:
            for i in range(4):
                tc = blk * 4 + i
                P.emit("sync",
                       lambda e, tc=tc, i=i, xn=xnb:
                       e.dma_start(out=s_xn[:, :, tc * 128:(tc + 1) * 128],
                                   in_=xn[:, i, :].rearrange("p (c n) -> p c n", n=128),
                                   transpose=True),
                       wait=[am] + first, inc=tg[par], inck=16)
                first = []
            xnm.append((tg[par], tg[par].n))
        return xnm

    def stage_A(x_src, t0, cpar):
        blks = [stage_A_blk(x_src, t0, blk, cpar, ()) for blk in range(NBLK)]
        return stage_A_xpose(blks, None)

    b_done = [None]
    g_marks = [[]]
    wu_pref = [None]
    gtm = [None]
    wfm_pending = [None]

    def load_B_smalls(smalls, gate):
        wgk1d, wgk2d, bgk2d = smalls
        pre = [(gd, gd.n), (te, te.n), (sc, sc.n)]
        g_dma(s_wgk1[:, :, :],
              wgk1d.rearrange("p (c n) -> p c n", n=LR), wait=pre)
        g_dma(s_wgk2[:, :, :], wgk2d.rearrange("k (c n) -> k c n", n=128))
        g_dma(s_bgk[:, :], bgk2d)
        if gate:
            g_dma(s_gateW[:, :, :],
                  gateW.rearrange("p (c n) -> p c n", n=NE))
        return gd.n

    def stage_B(gw, gate, xnm, epar=0, stop=None, mid=None):
        if stop == "Bs":
            return None, None
        if gate:
            for tc in range(NTC):
                for dc in range(NDC):
                    mm(pf2[:, tc * 4:(tc + 1) * 4],
                       s_xn[:, dc, tc * 128:(tc + 1) * 128], s_gateW[:, dc, :],
                       start=(dc == 0), stop=False,
                       wait=[(gd, gw), xnm[tc // 4]] if dc == 0 else [])
                mm(pf2[:, tc * 4:(tc + 1) * 4], ones1, s_gateb[:, :],
                   start=False, stop=True, inc=te if tc == NTC - 1 else None)
            gtm[0] = (te, te.n)
        for blk in range(NBLK):
            ps, w0 = rot.next()
            for dc in range(NDC):
                mm(ps[0:16, :], s_wgk1[:, dc, :],
                   s_xn[:, dc, blk * 512:(blk + 1) * 512],
                   start=(dc == 0), stop=(dc == NDC - 1),
                   wait=(w0 + [(gd, gw), xnm[blk]]) if dc == 0 else [],
                   inc=te if dc == NDC - 1 else None)
            v_op(lambda e, ps=ps, blk=blk: e.tensor_copy(
                out=s_gk1[:, blk * 512:(blk + 1) * 512], in_=ps[0:16, :]),
                wait=[(te, te.n)])
            rot.done(ve)
        if stop == "B0":
            return None, None
        for kc in range(NKC):
            for blk in range(NBLK):
                ps, w0 = rot.next()
                bsl = slice(blk * 512, (blk + 1) * 512)
                mm(ps[:, :], s_wgk2[:, kc, :], s_gk1[:, bsl], start=True,
                   stop=True,
                   wait=w0 + ([(ve, ve.n)] if kc == 0 and blk == 0 else []),
                   inc=te)
                s_op(lambda e, ps=ps, kc=kc, bsl=bsl: e.activation(
                    out=s_sg[:, kc, bsl], in_=ps[:, :], func=ACT.Sigmoid,
                    bias=s_bgk[:, kc:kc + 1]),
                    wait=[(te, te.n)] + ([(gd, gw)] if kc == 0 and blk == 0 else []))
                rot.done(sc)
        if gate:
            g = s_gsc
            st4 = NTC * 4
            v_op(lambda e: e.tensor_copy(out=g[:, 0:st4], in_=pf2[:, 0:st4]),
                 wait=[gtm[0]])

            def gsl_(i):
                return g[:, i:st4:4]
            v_op(lambda e: e.tensor_tensor(out=g[:, st4:st4 + NTC], in0=gsl_(0),
                 in1=gsl_(1), op=ALU.max), wait=[(ve, ve.n)])
            v_op(lambda e: e.tensor_tensor(out=g[:, st4 + 8:st4 + 8 + NTC], in0=gsl_(2),
                 in1=gsl_(3), op=ALU.max))
            v_op(lambda e: e.tensor_tensor(out=g[:, st4 + 16:st4 + 16 + NTC], in0=gsl_(0),
                 in1=gsl_(1), op=ALU.min))
            v_op(lambda e: e.tensor_tensor(out=g[:, st4 + 24:st4 + 24 + NTC], in0=gsl_(2),
                 in1=gsl_(3), op=ALU.min))
            v_op(lambda e: e.tensor_tensor(out=g[:, st4 + 32:st4 + 32 + NTC],
                 in0=g[:, st4:st4 + NTC], in1=g[:, st4 + 8:st4 + 8 + NTC], op=ALU.max))
            v_op(lambda e: e.tensor_tensor(out=g[:, st4 + 40:st4 + 40 + NTC],
                 in0=g[:, st4:st4 + NTC], in1=g[:, st4 + 8:st4 + 8 + NTC], op=ALU.min))
            v_op(lambda e: e.tensor_tensor(out=g[:, st4 + 48:st4 + 48 + NTC],
                 in0=g[:, st4 + 16:st4 + 16 + NTC], in1=g[:, st4 + 24:st4 + 24 + NTC],
                 op=ALU.max))
            v_op(lambda e: e.tensor_tensor(out=g[:, st4 + 56:st4 + 56 + NTC],
                 in0=g[:, st4 + 40:st4 + 40 + NTC], in1=g[:, st4 + 48:st4 + 48 + NTC],
                 op=ALU.max))
            for tc in range(NTC):
                v_op(lambda e, tc=tc: e.tensor_scalar(
                    out=g[:, tc * 4:(tc + 1) * 4], in0=g[:, tc * 4:(tc + 1) * 4],
                    scalar1=g[:, st4 + 32 + tc:st4 + 33 + tc], scalar2=None,
                    op0=ALU.subtract), wait=[(ve, ve.n)])
            v_op(lambda e: e.tensor_tensor(out=g[:, st4 + 40:st4 + 40 + NTC],
                 in0=g[:, st4 + 56:st4 + 56 + NTC], in1=g[:, st4 + 32:st4 + 32 + NTC],
                 op=ALU.subtract))
            s_op(lambda e: e.activation(out=g[:, 0:st4], in_=g[:, 0:st4],
                 func=ACT.Exp), wait=[(ve, ve.n)])
            s_op(lambda e: e.activation(out=g[:, st4 + 40:st4 + 40 + NTC],
                 in_=g[:, st4 + 40:st4 + 40 + NTC], func=ACT.Exp))
            for tc in range(NTC):
                v_op(lambda e, tc=tc: e.tensor_scalar(
                    out=s_wc[:, tc, :], in0=g[:, tc * 4:(tc + 1) * 4],
                    scalar1=g[:, st4 + 40 + tc:st4 + 41 + tc], scalar2=None,
                    op0=ALU.is_ge), wait=[(sc, sc.n)])
            v_op(lambda e: e.tensor_tensor(
                out=g[:, 0:st4], in0=g[:, 0:st4],
                in1=s_wc[:, :, :].rearrange("p a b -> p (a b)"), op=ALU.mult),
                wait=[(ve, ve.n)])
            v_op(lambda e: e.reduce_sum(
                out=g[:, st4 + 48:st4 + 48 + NTC],
                in_=g[:, 0:st4].rearrange("p (a b) -> p a b", b=4), axis=AXL.X),
                wait=[(ve, ve.n)])
            v_op(lambda e: e.reciprocal(out=g[:, st4 + 48:st4 + 48 + NTC],
                 in_=g[:, st4 + 48:st4 + 48 + NTC]))
            for tc in range(NTC):
                v_op(lambda e, tc=tc: e.tensor_scalar(
                    out=s_wc[:, tc, :], in0=g[:, tc * 4:(tc + 1) * 4],
                    scalar1=g[:, st4 + 48 + tc:st4 + 49 + tc], scalar2=None,
                    op0=ALU.mult), wait=[(ve, ve.n)])
            wfm_pending[0] = (ve, ve.n)
        if mid is not None:
            mid()
        eqms = []
        emms = []
        for hf in range(2):
            s_op(lambda e, hf=hf: e.activation(
                out=s_bb[:, hf * 4:(hf + 1) * 4, :], in_=s_sg[:, hf * 4:(hf + 1) * 4, :],
                func=ACT.Ln),
                wait=[(xd, xd.n), (ve, ve.n)] if hf == 0 else [])
            lnm = sc.n
            for kc in range(hf * 4, (hf + 1) * 4):
                for c in range(NCH):
                    v_op(lambda e, kc=kc, c=c: e.tensor_tensor_scan(
                        out=s_bb[:, kc, c * 128:(c + 1) * 128],
                        data0=c_os[:, :], data1=s_bb[:, kc, c * 128:(c + 1) * 128],
                        initial=0.0, op0=ALU.mult, op1=ALU.add),
                        wait=[(sc, lnm)] if (kc == hf * 4 and c == 0) else [])
            bbok_h = ve.n
            s_op(lambda e, hf=hf: e.activation(
                out=s_Em[:, hf * 4:(hf + 1) * 4, :], in_=s_bb[:, hf * 4:(hf + 1) * 4, :],
                func=ACT.Exp, scale=-1.0 / GN),
                wait=[(ve, bbok_h)] + ([(te, te.n), (tp, tp.n)] if hf == 0 else []))
            emms.append((sc, sc.n))
            s_op(lambda e, hf=hf: e.activation(
                out=s_eb[:, epar, hf * 4:(hf + 1) * 4, :],
                in_=s_bb[:, hf * 4:(hf + 1) * 4, 127::128],
                func=ACT.Exp, scale=1.0 / GN))
        for hf in range(2):
            s_op(lambda e, hf=hf: e.activation(
                out=s_Eq[:, hf * 4:(hf + 1) * 4, :], in_=s_bb[:, hf * 4:(hf + 1) * 4, :],
                func=ACT.Exp, scale=1.0 / GN))
            eqms.append((sc, sc.n))
        b_done[0] = (te, te.n)
        return eqms, emms

    def proj_fm(dst, Wd, gate_sb, eq_marks, jb_ok=None, evscale=None):
        first = True
        firstjb = [jb_ok] if jb_ok else []
        jbtt = [None, None]
        halfused = [False, False]
        slots = []
        gws = []
        for half in range(2):
            slots.append(wu_next(Wd[:, half * 512:(half + 1) * 512]))
            gws.append(gsl[slots[half]].n)
        bmarks = []
        for blk in range(NBLK):
            bsl = slice(blk * 512, (blk + 1) * 512)
            for kc in range(NKC):
                half, k2 = kc // 4, kc % 4
                slot = slots[half]
                ps, w0 = rot.next()
                for dc in range(NDC):
                    mm(ps[:, :],
                       slotv(slot, NDC, 512)[:, dc, k2 * 128:(k2 + 1) * 128],
                       s_xn[:, dc, bsl], start=(dc == 0), stop=(dc == NDC - 1),
                       wait=(w0 + [(gsl[slot], gws[half])] +
                             ([(ve, ve.n)] if first else [])) if dc == 0 else [],
                       inc=te if dc == NDC - 1 else None)
                first = False
                hj = (rot.i - 1) % 2
                jb = s_jb[:, hj * 512:(hj + 1) * 512]
                ew = []
                if not halfused[half]:
                    ew = [eq_marks[half]]
                    halfused[half] = True
                s_op(lambda e, ps=ps, jb=jb: e.activation(
                    out=jb, in_=ps[:, :], func=ACT.Identity,
                    scale=1.0 if evscale is None else evscale),
                    wait=[(te, te.n)] + ew + firstjb +
                         ([jbtt[hj]] if jbtt[hj] is not None else []))
                firstjb = []
                rot.done(sc)
                v_op(lambda e, jb=jb, kc=kc, bsl=bsl: e.tensor_tensor(
                    out=dst[:, kc, bsl], in0=jb, in1=gate_sb[:, kc, bsl],
                    op=ALU.mult), wait=[(sc, sc.n)])
                jbtt[hj] = (ve, ve.n)
            bmarks.append((ve, ve.n))
        wu_done(slots[0])
        wu_done(slots[1])
        return bmarks

    def proj_tm(dst, Wd, act, vm, pre_slot=None, ev_w=(), ded=None):
        first = True
        evw = list(ev_w)
        for q4 in range(4):
            if q4 == 0 and ded is not None:
                wview, gwait = ded
                slot = None
            else:
                slot = pre_slot if (q4 == 0 and pre_slot is not None) \
                    else wu_next(Wd[:, q4 * 512:(q4 + 1) * 512])
                wview = slotv(slot, NDC, 512)
                gwait = (gsl[slot], gsl[slot].n)
            for tc in range(NTC):
                ps, w0 = rot.next()
                for dc in range(NDC):
                    mm(ps[:, :], s_xn[:, dc, tc * 128:(tc + 1) * 128],
                       wview[:, dc, :],
                       start=(dc == 0), stop=(dc == NDC - 1),
                       wait=(w0 + [gwait] +
                             ([vm] if first else [])) if dc == 0 else [],
                       inc=te if dc == NDC - 1 else None)
                first = False
                if act is None:
                    v_op(lambda e, ps=ps, tc=tc, q4=q4: e.tensor_copy(
                        out=dst[:, tc, q4 * 512:(q4 + 1) * 512], in_=ps[:, :]),
                        wait=[(te, te.n)])
                    rot.done(ve)
                else:
                    s_op(lambda e, ps=ps, tc=tc, q4=q4: e.activation(
                        out=dst[:, tc, q4 * 512:(q4 + 1) * 512], in_=ps[:, :],
                        func=act), wait=[(te, te.n)] + evw)
                    evw = []
                    rot.done(sc)
            if slot is not None:
                wu_done(slot)

    def stage_C(prefix, eq_marks, em_marks, pre=(None, None),
                ded=(None, None), epar=0):
        vm = (te, te.n)
        proj_tm(s_v, Wv_p if prefix else Wv, None, vm, pre_slot=pre[0],
                ded=ded[0])
        if not prefix:
            proj_tm(s_silu, Wg, ACT.Silu, vm, pre_slot=pre[1],
                    ev_w=g_marks[0], ded=ded[1])
        qbm = None
        if not prefix:
            qbm = proj_fm(s_qg, Wq, s_Eq, eq_marks, jb_ok=eq_marks[1],
                          evscale=1.0 / 16.0)
        kbm = proj_fm(s_kd, Wk_p if prefix else Wk, s_Em, em_marks,
                      jb_ok=None if not prefix else eq_marks[1])
        tpms = []
        for kc in range(NKC):
            prev_tp = tp.n
            for c in range(NCH):
                v_op(lambda e, kc=kc, c=c: e.tensor_scalar(
                    out=s_kgs[:, c * 128:(c + 1) * 128],
                    in0=s_kd[:, kc, c * 128:(c + 1) * 128],
                    scalar1=s_eb[:, epar, kc, c:c + 1], scalar2=None,
                    op0=ALU.mult),
                    wait=[(ve, ve.n), (tp, prev_tp)] +
                         ([eq_marks[1], (tg[0], tg[0].n), (tg[1], tg[1].n)]
                          if (kc == 0 and c == 0) else []))
            t_dma(s_kgT[:, :, kc, :],
                  s_kgs[:, 0:NCH * 128].rearrange("p (c n) -> p c n", n=128),
                  wait=[(ve, ve.n), (te, te.n), (sc, sc.n)])
            tpms.append((tp, tp.n))
        return tpms, qbm, kbm

    def stage_DE(prefix, tpms, qbm=None, kbm=None, epar=0):
        tp0 = tp.n
        xd0 = xd.n
        ebm = sc.n
        rms_done = [None]
        if not prefix:
            slots = []
            gws = []
            for q in range(4):
                slots.append(wu_next(Wo[q * 512:(q + 1) * 512, :]))
                gws.append(gsl[slots[q]].n)
            for q in range(4):
                for vci in range(4):
                    s_op(lambda e, q=q, vci=vci: e.activation(
                        out=slotv(slots[q], 4, 1024)[:, vci, :],
                        in_=slotv(slots[q], 4, 1024)[:, vci, :],
                        func=ACT.Identity,
                        scale=s_rms[:, (q * 4 + vci) % 4:(q * 4 + vci) % 4 + 1]),
                        wait=[(gsl[slots[q]], gws[q])])
            rms_done[0] = sc.n
        v_done = ve.n
        sil_done = sc.n
        c_done = [(ve, ve.n), (sc, sc.n)]
        f2m = None
        f1m = None
        scpy = None
        dg_te = [None] * NKC

        def diag_build(c):
            ms = []
            for r in range(NKC):
                w0 = [] if dg_te[r] is None else [dg_te[r]]
                if c == 0 and r == 0:
                    w0 = w0 + [(sc, ebm)]
                v_op(lambda e, r=r, c=c: e.tensor_scalar(
                    out=s_diag[:, r, :], in0=c_idb[:, :],
                    scalar1=s_eb[:, epar, r, c:c + 1],
                    scalar2=None, op0=ALU.mult), wait=w0)
                ms.append((ve, ve.n))
            return ms

        dgv = diag_build(0)
        for c in range(NCH):
            if not prefix:
                for h in range(H):
                    for k in range(2):
                        kc = 2 * h + k
                        w0 = []
                        if h == 0 and k == 0:
                            w0 = [f2m] if c > 0 else [wfm_pending[0], (sc, ebm)]
                            if c % 4 == 0:
                                w0 = w0 + [qbm[c // 4], kbm[c // 4]]
                            if c == 0:
                                w0 = [x for x in w0 if x is not None]
                        mm(pf2[:, h * 128:(h + 1) * 128],
                           s_kd[:, kc, c * 128:(c + 1) * 128],
                           s_qg[:, kc, c * 128:(c + 1) * 128],
                           start=(k == 0), stop=(k == 1), wait=w0,
                           inc=te if k == 1 else None)
                atm = te.n
                for h in range(H):
                    v_op(lambda e, h=h: e.tensor_tensor(
                        out=s_AT[:, h, :], in0=pf2[:, h * 128:(h + 1) * 128],
                        in1=c_mU[:, :], op=ALU.mult),
                        wait=[(te, atm)])
                f2m = (ve, ve.n)
                atvm = ve.n
                cpym = []
                for h in range(H):
                    po, w0 = rot.next()
                    w0 = w0 + [(ve, atvm)]
                    if c > 0 and h == 0 and scpy is not None:
                        w0.append(scpy)
                    mm(po[:, :], s_AT[:, h, :], s_v[:, c, h * 512:(h + 1) * 512],
                       start=True, stop=False, wait=w0)
                    for k in range(2):
                        kc = 2 * h + k
                        mm(po[:, :], s_qg[:, kc, c * 128:(c + 1) * 128],
                           s_S[:, kc, :], start=False, stop=(k == 1),
                           inc=te if k == 1 else None)
                    teo = te.n
                    v_op(lambda e, po=po, h=h: e.tensor_copy(
                        out=s_ogc[:, h * 512:(h + 1) * 512], in_=po[:, :]),
                        wait=[(te, teo)] + ([tpms[-1]] if (c == 0 and h == 0)
                                            else []))
                    rot.done(ve)
                    cpym.append(ve.n)
                    s_op(lambda e, h=h: e.activation(
                        out=s_jb[:, 0:512], in_=s_ogc[:, h * 512:(h + 1) * 512],
                        func=ACT.Square, accum_out=s_gsc[:, 80 + h:81 + h]),
                        wait=[(ve, cpym[h])] + ([(xd, xd0)] if (c == 0 and h == 0)
                                                else []))
            def state_rows(c=c):
                nonlocal scpy, dgv
                for r in range(NKC):
                    rr = rot2p if (prefix and r % 2 == 1) else rot2
                    pS, w2 = rr.next()
                    mm(pS[:, :], s_diag[:, r, :], s_S[:, r, :], start=True,
                       stop=False, wait=w2 + [dgv[r]])
                    mm(pS[:, :], s_kgT[:, c, r, :],
                       s_v[:, c, (r // 2) * 512:(r // 2 + 1) * 512],
                       start=False, stop=True,
                       wait=[tpms[r]] if c == 0 else [], inc=te)
                    dg_te[r] = (te, te.n)
                    s_op(lambda e, pS=pS, r=r: e.activation(
                        out=s_S[:, r, :], in_=pS[:, :], func=ACT.Identity),
                        wait=[dg_te[r]])
                    rr.done(sc)
                scpy = (sc, sc.n)
                if c + 1 < NCH:
                    dgv = diag_build(c + 1)
            if prefix:
                state_rows()
            if not prefix:
                s_op(lambda e: e.activation(
                    out=s_gsc[:, 84:88], in_=s_gsc[:, 80:84],
                    func=ACT.Ln, scale=1.0 / HDV, bias=s_cv[:, 0:1]))
                s_op(lambda e: e.activation(
                    out=s_gsc[:, 84:88], in_=s_gsc[:, 84:88],
                    func=ACT.Exp, scale=-0.5))
                scm = sc.n
                tph2 = []
                for h in range(H):
                    v_op(lambda e, h=h: e.tensor_scalar(
                        out=s_ogc[:, h * 512:(h + 1) * 512],
                        in0=s_ogc[:, h * 512:(h + 1) * 512],
                        scalar1=s_gsc[:, 84 + h:85 + h], scalar2=None,
                        op0=ALU.mult), wait=[(sc, scm)] if h == 0 else [])
                    v_op(lambda e, h=h, c=c: e.tensor_tensor(
                        out=s_ogc[:, h * 512:(h + 1) * 512],
                        in0=s_ogc[:, h * 512:(h + 1) * 512],
                        in1=s_silu[:, c, h * 512:(h + 1) * 512], op=ALU.mult))
                    if h == 1 or h == 3:
                        half = h // 2
                        P.emit("sync",
                               lambda e, half=half:
                               e.dma_start(out=s_ogf[:, half * 8:(half + 1) * 8, :],
                                           in_=s_ogc[:, half * 1024:(half + 1) * 1024]
                                           .rearrange("p (c n) -> p c n", n=128),
                                           transpose=True),
                               wait=[(ve, ve.n)] +
                                    ([wom] if c > 0 else
                                     ([(te, te.n)] if half == 0 else [])),
                               inc=tg[half], inck=16)
                        tph2.append((tg[half], tg[half].n))
                for q in range(4):
                    for vci in range(4):
                        vc = q * 4 + vci
                        for dmh in range(2):
                            w0 = []
                            if vc == 0 and dmh == 0:
                                w0 = [tph2[0]]
                                if c == 0:
                                    w0.append((sc, rms_done[0]))
                                if f1m is not None:
                                    w0.append(f1m)
                            elif vc == 8 and dmh == 0:
                                w0 = [tph2[1]]
                            mm(pf1[:, dmh * 512:(dmh + 1) * 512], s_ogf[:, vc, :],
                               slotv(slots[q], 4, 1024)[:, vci,
                                                        dmh * 512:(dmh + 1) * 512],
                               start=(vc == 0), stop=(vc == 15), wait=w0,
                               inc=te if (vc == 15 and dmh == 1) else None)
                wom = (te, te.n)
                v_op(lambda e, c=c: e.tensor_copy(out=s_attn[:, c, :],
                     in_=pf1[:, :]), wait=[wom])
                f1m = (ve, ve.n)
                state_rows()
        if not prefix:
            for q in range(4):
                wu_done(slots[q])

    def ln_cols(par, wcol, prev_colread):
        q = s_mcol[:, par, :, :]
        v_op(lambda e, q=q, par=par: e.tensor_tensor(
            out=q[:, 0, :], in0=s_scol[:, par, :, 0], in1=s_scol[:, par, :, 1],
            op=ALU.add), wait=[(sc, sc.n)])
        scolread = (ve, ve.n)
        v_op(lambda e, q=q: e.tensor_scalar(out=q[:, 0, :], in0=q[:, 0, :],
             scalar1=1.0 / D, scalar2=None, op0=ALU.mult))
        v_op(lambda e, q=q: e.tensor_tensor(out=q[:, 1, :], in0=q[:, 0, :],
             in1=q[:, 0, :], op=ALU.mult))
        v_op(lambda e, q=q, par=par: e.scalar_tensor_tensor(
            out=q[:, 1, :], in0=s_ssq[:, par, 0:NTC], scalar=1.0 / D,
            in1=q[:, 1, :], op0=ALU.mult, op1=ALU.subtract))
        ssqread = (ve, ve.n)
        s_op(lambda e, q=q: e.activation(out=q[:, 2, :], in_=q[:, 1, :],
             func=ACT.Ln, bias=s_cv[:, 0:1]), wait=[(ve, ve.n)])
        s_op(lambda e, q=q: e.activation(out=q[:, 2, :], in_=q[:, 2, :],
             func=ACT.Exp, scale=-0.5))
        if wcol is not None:
            for tc in range(NTC):
                v_op(lambda e, q=q, tc=tc: e.tensor_scalar(
                    out=q[:, 2, tc:tc + 1], in0=q[:, 2, tc:tc + 1],
                    scalar1=s_wc[:, tc, wcol:wcol + 1], scalar2=None,
                    op0=ALU.mult), wait=[(sc, sc.n)])
        v_op(lambda e, q=q: e.scalar_tensor_tensor(
            out=q[:, 3, :], in0=q[:, 0, :], scalar=-1.0, in1=q[:, 2, :],
            op0=ALU.mult, op1=ALU.mult), wait=[(sc, sc.n), (ve, ve.n)])
        return scolread, ssqread, (ve, ve.n)

    def z_phase(srcv, dstv, slot_a, slot_b, brow, par, evac_gate, first_w,
                sq_gate, colm):
        gwa, gwb = gsl[slot_a].n, gsl[slot_b].n
        evms = []
        for tc in range(NTC):
            for half in range(2):
                slot = slot_a if half == 0 else slot_b
                gw = gwa if half == 0 else gwb
                ps, w0 = rot.next()
                for dc in range(NDC):
                    w1_ = []
                    if dc == 0:
                        w1_ = w0 + [(gsl[slot], gw)]
                        if tc == 0 and half == 0:
                            w1_ += first_w
                    mm(ps[:, :], srcv[:, dc, tc * 128:(tc + 1) * 128],
                       slotv(slot, NDC, 512)[:, dc, :],
                       start=(dc == 0), stop=False, wait=w1_)
                mm(ps[:, :], ones1, brow[0:1, half * 512:(half + 1) * 512],
                   start=False, stop=True, inc=te)
                w2 = [(te, te.n)]
                if evac_gate is not None:
                    w2.append(evac_gate[tc])
                if tc == 0 and half == 0 and colm is not None:
                    w2.append(colm)
                s_op(lambda e, ps=ps, tc=tc, half=half: e.activation(
                    out=dstv[:, tc, half * 512:(half + 1) * 512], in_=ps[:, :],
                    func=ACT.Identity,
                    accum_out=s_scol[:, par, tc, half:half + 1]), wait=w2)
                rot.done(sc)
            w3 = [(sc, sc.n)]
            if tc == 0 and sq_gate is not None:
                w3.append(sq_gate)
            s_op(lambda e, tc=tc: e.activation(out=s_jb[:, :], in_=dstv[:, tc, :],
                 func=ACT.Square, accum_out=s_ssq[:, par, tc:tc + 1]), wait=w3)
            evms.append((sc, sc.n))
        wu_done(slot_a)
        wu_done(slot_b)
        return evms, (te, te.n)

    def apply1(e, zv, hv, par, relu_gate, p2gate, p3gate):
        marks = []
        p2m = p3m = None
        for tc in range(NTC):
            w1_ = [(ve, ve.n)]
            if tc == 0:
                w1_.append((gbc, gbc.n))
            v_op(lambda e_, tc=tc, zv=zv: e_.tensor_scalar(
                out=zv[:, tc, :], in0=zv[:, tc, :],
                scalar1=s_mcol[:, par, 2, tc:tc + 1],
                scalar2=s_mcol[:, par, 3, tc:tc + 1],
                op0=ALU.mult, op1=ALU.add), wait=w1_)
            w2_ = [p2gate] if (tc == 0 and p2gate is not None) else []
            v_op(lambda e_, tc=tc, zv=zv: e_.tensor_tensor(
                out=zv[:, tc, :], in0=zv[:, tc, :], in1=s_bc1[:, :],
                op=ALU.mult), wait=w2_)
            if tc == NTC - 1:
                p2m = (ve, ve.n)
            w3_ = [p3gate] if (tc == 0 and p3gate is not None) else []
            v_op(lambda e_, tc=tc, zv=zv: e_.tensor_tensor(
                out=zv[:, tc, :], in0=zv[:, tc, :], in1=s_bc2[:, :],
                op=ALU.add), wait=w3_)
            if tc == NTC - 1:
                p3m = (ve, ve.n)
            w4_ = [relu_gate] if (tc == 0 and relu_gate is not None) else []
            v_op(lambda e_, tc=tc, zv=zv, hv=hv: e_.tensor_scalar(
                out=hv[:, tc, :], in0=zv[:, tc, :], scalar1=0.0, scalar2=None,
                op0=ALU.max), wait=w4_)
            marks.append((ve, ve.n))
        return marks, p2m, p3m

    def apply2_mix(e, zv, par, p2gate):
        marks = []
        p2m = None
        for tc in range(NTC):
            w1_ = [(ve, ve.n)]
            if tc == 0:
                w1_.append((gbc, gbc.n))
            v_op(lambda e_, tc=tc, zv=zv: e_.tensor_scalar(
                out=zv[:, tc, :], in0=zv[:, tc, :],
                scalar1=s_mcol[:, par, 2, tc:tc + 1],
                scalar2=s_mcol[:, par, 3, tc:tc + 1],
                op0=ALU.mult, op1=ALU.add), wait=w1_)
            w2_ = [p2gate] if (tc == 0 and p2gate is not None) else []
            v_op(lambda e_, tc=tc, zv=zv: e_.tensor_tensor(
                out=zv[:, tc, :], in0=zv[:, tc, :], in1=s_bc1[:, :],
                op=ALU.mult), wait=w2_)
            if tc == NTC - 1:
                p2m = (ve, ve.n)
            v_op(lambda e_, tc=tc, zv=zv: e_.tensor_tensor(
                out=s_attn[:, tc, :], in0=s_attn[:, tc, :], in1=zv[:, tc, :],
                op=ALU.add))
            marks.append((ve, ve.n))
        return marks, p2m

    def stage_F(side=None):
        for tc in range(NTC):
            P.emit("tensor", lambda e, tc=tc: e.transpose(
                out=pf1[0:4, 0:128], in_=s_wc[:, tc, :], identity=c_idf[:, :]),
                wait=([wfm_pending[0]] if tc == 0 else []) + [(ve, ve.n)], inc=te)
            v_op(lambda e, tc=tc: e.tensor_copy(out=s_wfm[0:4, tc, :],
                 in_=pf1[0:4, 0:128]), wait=[(te, te.n)])
        g_dma(s_be2[:, :], be2r[:, :], wait=[(te, te.n), (gd, gd.n)])
        be2m = gd.n
        for tc in range(NTC):
            for half in range(2):
                ps, w0 = rot.next()
                mm(ps[:, :], s_wfm[0:4, tc, :],
                   s_be2[0:4, half * 512:(half + 1) * 512],
                   start=True, stop=True,
                   wait=w0 + ([(gd, be2m)]
                              if tc == 0 and half == 0 else []), inc=te)
                v_op(lambda e, ps=ps: e.tensor_copy(out=s_jb[:, 0:512],
                     in_=ps[:, :]),
                     wait=[(te, te.n)] + ([(sc, sc.n)] if tc == 0 and half == 0
                                          else []))
                rot.done(ve)
                v_op(lambda e, tc=tc, half=half: e.tensor_tensor(
                    out=s_attn[:, tc, half * 512:(half + 1) * 512],
                    in0=s_attn[:, tc, half * 512:(half + 1) * 512],
                    in1=s_jb[:, 0:512], op=ALU.add))
        s_k3 = s_kgT[:, :, :, :].rearrange("p a b n -> p a (b n)")
        jbm = (ve, ve.n)
        phases = [("z1", 0), ("z1", 1), ("z2", 0), ("z1", 2), ("z2", 1),
                  ("z1", 3), ("z2", 2), ("z2", 3)]
        F0ev = [(te, te.n)]
        st_ = {e: {} for e in range(NE)}
        last_colread = [None, None]
        last_ssqread = [jbm, jbm]
        last_p2 = [None]
        last_p3 = [None]

        prep = {}

        def prepare(pi):
            if pi >= len(phases) or pi in prep:
                return
            kind, e = phases[pi]
            par = pi % 2
            if kind == "z1":
                ra = s_rb[0:1, 0:1024] if par == 0 else s_rb2[0:1, :]
                g_dma(ra, b1r[e:e + 1, :], sem=gbr2[par], wait=[(te, te.n)])
                sa = wu_next(W1[e, :, 0:512])
                sb = wu_next(W1[e, :, 512:1024])
            else:
                ra = (s_rb[0:1, 1024:2048] if par == 0 else
                      s_diag[0:1, :, :].rearrange("p a b -> p (a b)")[0:1, 0:1024])
                g_dma(ra, b2r[e:e + 1, :], sem=gbr2[par], wait=[(te, te.n)])
                sa = wu_next(W2[e, :, 0:512])
                sb = wu_next(W2[e, :, 512:1024])
            prep[pi] = (ra, (gbr2[par], gbr2[par].n), sa, sb)

        def emit_phase(pi):
            kind, e = phases[pi]
            par = pi % 2
            ra, gbm, sa, sb = prep[pi]
            if kind == "z1":
                egate = None if e == 0 else st_[e - 1].get("apply1")
                first_w = [gbm]
                colm0 = last_colread[par]
                if pi == 0:
                    colm0 = F0ev[0]
                evms, tem = z_phase(s_xn, s_z, sa, sb, ra, par, egate, first_w,
                                    last_ssqread[par], colm0)
                st_[e]["z1ev"] = evms
                st_[e]["z1te"] = tem
                st_[e]["par1"] = par
            else:
                egate = None if e == 0 else st_[e - 1].get("mix")
                first_w = [gbm, st_[e]["tph"]]
                evms, tem = z_phase(s_hfm, s_k3, sa, sb, ra, par, egate, first_w,
                                    last_ssqread[par], last_colread[par])
                st_[e]["z2ev"] = evms
                st_[e]["z2te"] = tem
                st_[e]["par2"] = par

        def emit_tail(kind, e):
            if kind == "z1":
                par = st_[e]["par1"]
                g_dma(s_bc1[:, :], g1b[e, :, :], sem=gbc,
                      wait=[last_p2[0]] if last_p2[0] else [(ve, ve.n)])
                g_dma(s_bc2[:, :], be1b[e, :, :], sem=gbc,
                      wait=[last_p3[0]] if last_p3[0] else [])
                cr, sr, am = ln_cols(par, None, None)
                last_colread[par], last_ssqread[par] = cr, sr
                relu_gate = None if e == 0 else st_[e - 1]["z2te"]
                marks, p2m, p3m = apply1(e, s_z, s_h, par, relu_gate, None, None)
                st_[e]["apply1"] = marks
                last_p2[0] = p2m
                last_p3[0] = p3m
                for tc in range(NTC):
                    t_dma(s_hfm[:, :, tc * 128:(tc + 1) * 128],
                          s_h[:, tc, :].rearrange("p (c n) -> p c n", n=128),
                          wait=[marks[tc]] + ([st_[e - 1]["z2te"]] if (e > 0 and tc == 0)
                                              else ([(te, te.n)] if tc == 0 else [])))
                st_[e]["tph"] = (tp, tp.n)
            else:
                par = st_[e]["par2"]
                g_dma(s_bc1[:, :], g2b[e, :, :], sem=gbc,
                      wait=[last_p2[0]] if last_p2[0] else [(ve, ve.n)])
                cr, sr, am = ln_cols(par, e, None)
                last_colread[par], last_ssqread[par] = cr, sr
                marks, p2m = apply2_mix(e, s_k3, par, None)
                st_[e]["mix"] = marks
                last_p2[0] = p2m

        for pi, (kind, e) in enumerate(phases):
            prepare(pi)
            prepare(pi + 1)
            emit_phase(pi)
            if kind == "z1" and e == 0:
                emit_tail("z1", 0)
            elif kind == "z2":
                if e + 1 < NE:
                    emit_tail("z1", e + 1)
                emit_tail("z2", e)
            if side is not None:
                side(pi)

    def stage_G(t0, cpar):
        gxm = [None, None]
        for tc in range(NTC):
            sl = tc % 2
            rows = x_tm[t0 + tc * 128: t0 + (tc + 1) * 128, :]
            x_dma(s_gx[:, sl, :].rearrange("p (c n) -> p c n", n=D),
                  rows.rearrange("(c p) n -> p c n", p=128),
                  wait=([gxm[sl]] if gxm[sl] else
                        [(te, te.n), (ve, ve.n), (sc, sc.n)]))
            s_op(lambda e, tc=tc: e.activation(
                out=s_cf[:, :], in_=s_attn[:, tc, :], func=ACT.Identity,
                bias=s_cols[:, cpar, 6, tc:tc + 1]),
                wait=[(ve, ve.n)])
            v_op(lambda e, sl=sl, tc=tc: e.scalar_tensor_tensor(
                out=s_jf[:, :], in0=s_gx[:, sl, :],
                scalar=s_cols[:, cpar, 7, tc:tc + 1],
                in1=s_cf[:, :], op0=ALU.mult, op1=ALU.add),
                wait=[(xd, xd.n), (ve, ve.n), (sc, sc.n)])
            gxm[sl] = (ve, ve.n)
            x_dma(out_d[t0 + tc * 128: t0 + (tc + 1) * 128, :], s_jf[:, :],
                  wait=[(ve, ve.n)])
        g_marks[0] = [(ve, ve.n), (xd, xd.n)]

    stop = cfg.stop_after
    stop_st = 0
    if stop and "@" in stop:
        stop, sst = stop.split("@")
        stop_st = int(sst)
    cpar_of = {}
    gi = 0
    for st in range(cfg.NPST):
        cpar_of[("p", st)] = gi % 2
        gi += 1
    for st in range(cfg.NST):
        cpar_of[("m", st)] = gi % 2
        gi += 1
    xnm = stage_A(xp_tm, 0, cpar_of[("p", 0)])
    pend_pDE = [None]

    def emit_pend_pDE():
        if pend_pDE[0] is not None:
            stage_DE(True, pend_pDE[0][0], epar=pend_pDE[0][1])
            pend_pDE[0] = None
    for st in range(cfg.NPST):
        cpar = cpar_of[("p", st)]
        pv = wu_next(Wv_p[:, 0:512])
        gw_ = load_B_smalls((Wgk1_p, Wgk2_p, bgk2c_p), gate=False)
        r_ = stage_B(gw_, gate=False, xnm=xnm, epar=cpar)
        eqm, emm = r_
        tpms, qbm_, kbm_ = stage_C(True, eqm, emm, pre=(pv, None), epar=cpar)
        sqw = [(ve, ve.n)]
        if st + 1 < cfg.NPST:
            nsrc, nt0, ncp = xp_tm, (st + 1) * ST, cpar_of[("p", st + 1)]
        else:
            nsrc, nt0, ncp = x_tm, 0, cpar_of[("m", 0)]
        blks = [stage_A_blk(nsrc, nt0, b, ncp, (), sq_w=sqw)
                for b in range(NBLK)]
        xnm = stage_A_xpose(blks, (te, te.n))
        if st == cfg.NPST - 1:
            g_dma(s_w4[:, :, :], _rearr(Wv[:, 0:512]),
                  wait=[(ve, ve.n), (sc, sc.n), (te, te.n)], sem=gs4)
            g_dma(s_w5[:, :, :], _rearr(Wg[:, 0:512]),
                  wait=[(tg[0], tg[0].n), (te, te.n)], sem=gs5)
            wu_pref[0] = ((s_w4, (gs4, gs4.n)), (s_w5, (gs5, gs5.n)))
        stage_DE(True, tpms, epar=cpar)
    pend_G = [None]

    def emit_pend_G():
        emit_pend_pDE()
        if pend_G[0] is not None:
            stage_G(*pend_G[0])
            pend_G[0] = None
    for st in range(cfg.NST):
        t0 = st * ST
        cpar = cpar_of[("m", st)]
        if wu_pref[0] is not None:
            dedv = wu_pref[0]
            wu_pref[0] = None
            pv = pg2 = None
        else:
            dedv = (None, None)
            pv = wu_next(Wv[:, 0:512])
            pg2 = wu_next(Wg[:, 0:512])
        gw_ = load_B_smalls((Wgk1, Wgk2, bgk2c), gate=True)
        if stop == "A" and st >= stop_st:
            break
        r_ = stage_B(gw_, gate=True, xnm=xnm,
                     stop=stop if (stop in ("Bw1", "Bw2", "Bw3", "Bs", "B0", "B1",
                                            "B2", "B3", "B4") and st >= stop_st)
                     else None, mid=emit_pend_G, epar=cpar)
        if stop in ("B", "Bw1", "Bw2", "Bw3", "Bs", "B0", "B1", "B2", "B3", "B4") \
                and st >= stop_st:
            break
        eqm, emm = r_
        tpms, qbm_, kbm_ = stage_C(False, eqm, emm, pre=(pv, pg2), ded=dedv,
                                   epar=cpar)
        if stop == "C" and st >= stop_st:
            break
        stage_DE(False, tpms, qbm_, kbm_, epar=cpar)
        if stop == "D" and st >= stop_st:
            break
        side = None
        if st + 1 < cfg.NST:
            demark = [(te, te.n), (ve, ve.n)]
            nt0, ncp = (st + 1) * ST, cpar_of[("m", st + 1)]
            nA = [None, None]
            xnm_n = [None]

            a0 = [None]

            def side(pi, demark=demark, nt0=nt0, ncp=ncp, nA=nA, xnm_n=xnm_n,
                     a0=a0):
                if pi == 1:
                    nA[0] = stage_A_blk(x_tm, nt0, 0, ncp, demark)
                    a0[0] = [(ve, ve.n), (sc, sc.n), (te, te.n)]
                elif pi == 3:
                    if NBLK > 1:
                        nA[1] = stage_A_blk(x_tm, nt0, 1, ncp, demark)
                    g_dma(s_w4[:, :, :], _rearr(Wv[:, 0:512]),
                          wait=a0[0], sem=gs4)
                elif pi == 5:
                    xnm_n[0] = stage_A_xpose(nA[:NBLK], (te, te.n))
                    g_dma(s_w5[:, :, :], _rearr(Wg[:, 0:512]),
                          wait=[(tg[0], tg[0].n), (te, te.n)], sem=gs5)
                    wu_pref[0] = ((s_w4, (gs4, gs4.n)),
                                  (s_w5, (gs5, gs5.n)))
        stage_F(side=side)
        if side is not None:
            xnm = xnm_n[0]
        if stop == "F" and st >= stop_st:
            break
        pend_G[0] = (t0, cpar)
    emit_pend_G()

    if cfg.debug:
        allw = [(te, te.n), (ve, ve.n), (sc, sc.n), (tp, tp.n), (gd, gd.n)]
        valid = {"A": ["xn", "cols"],
                 "Bs": ["xn", "cols"], "B0": ["xn", "cols"],
                 "Bw1": ["xn", "cols"], "Bw2": ["xn", "cols"],
                 "Bw3": ["xn", "cols"],
                 "Bm": ["xn", "cols"], "Bsg": ["xn", "cols"],
                 "Bln": ["xn", "cols"],
                 "B1": ["xn", "cols", "bb"],
                 "B2": ["xn", "cols", "bb", "eb"],
                 "B3": ["xn", "cols", "bb", "eb"],
                 "B4": ["xn", "cols", "bb", "eb", "w"],
                 "B": ["xn", "cols", "bb", "eb", "w"],
                 "C": ["xn", "cols", "bb", "eb", "w", "qg", "kd", "kgT", "v",
                       "silu"],
                 "D": ["xn", "cols", "eb", "w", "qg", "kd", "kgT", "v", "silu",
                       "S", "attn"],
                 "F": ["xn", "cols", "eb", "w", "qg", "kd", "kgT",
                       "S", "attn"],
                 None: ["xn", "cols", "eb", "w", "qg", "kd", "kgT",
                        "S", "attn"]}[stop]
        pairs = [("xn", s_xn[:, :, :]), ("bb", s_bb[:, :, :]),
                 ("eb", s_eb[:, 0, :, :]), ("qg", s_qg[:, :, :]),
                 ("kd", s_kd[:, :, :]), ("kgT", s_kgT[:, :, :, :]),
                 ("v", s_v[:, :, :]), ("silu", s_silu[:, :, :]),
                 ("S", s_S[:, :, :]), ("attn", s_attn[:, :, :]),
                 ("w", s_wc[:, :, :].rearrange("p a b -> p (a b)")),
                 ("cols", s_cols[:, 0, :, :])]
        first = True
        for nm, ap in pairs:
            if nm in dbg and nm in valid:
                x_dma(dbg[nm], ap, wait=allw if first else [])
                first = False

    P.emit("sync", lambda e: e.nop(),
           wait=[(xd, xd.n), (gd, gd.n), (tp, tp.n), (gx, gx.n), (gbc, gbc.n),
                 (gbr, gbr.n)] + [(s, s.n) for s in gsl] + [(s, s.n) for s in tg]
                + [(s, s.n) for s in gbr2])

    with nc.Block() as block:
        @block.sync
        def _(eng):
            for f in P.q["sync"]:
                f(eng)

        @block.scalar
        def _(eng):
            for f in P.q["scalar"]:
                f(eng)

        @block.vector
        def _(eng):
            for f in P.q["vector"]:
                f(eng)

        @block.tensor
        def _(eng):
            for f in P.q["tensor"]:
                f(eng)

        @block.gpsimd
        def _(eng):
            for f in P.q["gpsimd"]:
                f(eng)

    for cm in sem_ctx:
        cm.__exit__(None, None, None)
    return nc


def _consts():
    import ml_dtypes
    identf = np.eye(128, dtype=np.float32)
    identb = np.eye(128).astype(ml_dtypes.bfloat16)
    maskU = np.triu(np.ones((128, 128), dtype=np.float32))
    onesrow = np.ones((1, 512), dtype=ml_dtypes.bfloat16)
    onescan = np.ones((128, 128), dtype=np.float32)
    cvals = np.zeros((128, 3), np.float32)
    cvals[:, 0] = EPS
    cvals[:, 1] = LN16
    cvals[:, 2] = 1.0
    return identf, identb, maskU, onesrow, onescan, cvals


def _pk(w):
    n = w.shape[1]
    return np.ascontiguousarray(
        w.reshape(NDC, 128, n).transpose(1, 0, 2).reshape(128, NDC * n))


def make_inputs(core, cfg, inp):
    x = np.asarray(inp["L0_attn_x"])
    b, hh = core // 2, core % 2
    T = cfg.T
    xc = np.ascontiguousarray(x[b, hh * T:(hh + 1) * T, :], dtype=np.float32)
    xp = np.ascontiguousarray(x[b, 0:cfg.P, :], dtype=np.float32)
    identf, identb, maskU, onesrow, onescan, cvals = _consts()
    rms = np.asarray(inp["L0_attn_rms_w"], np.float32)
    rms_c = np.ascontiguousarray(rms.reshape(4, 128).T)
    g1 = np.asarray(inp["L0_ffn_g1"], np.float32)
    be1 = np.asarray(inp["L0_ffn_be1"], np.float32)
    g2 = np.asarray(inp["L0_ffn_g2"], np.float32)
    bgk2 = np.asarray(inp["L0_attn_bgk2"], np.float32)
    bgk2c = np.ascontiguousarray(bgk2.reshape(NKC, 128).T)

    def bc(a):
        return np.ascontiguousarray(np.broadcast_to(a[:, None, :], (NE, 128, D)))

    z = np.zeros
    im = {
        "x_tm": xc, "xp_tm": xp,
        "Wq": np.ascontiguousarray(inp["L0_attn_Wq"], np.float32),
        "Wk": np.ascontiguousarray(inp["L0_attn_Wk"], np.float32),
        "Wv": np.ascontiguousarray(inp["L0_attn_Wv"], np.float32),
        "Wg": np.ascontiguousarray(inp["L0_attn_Wg"], np.float32),
        "Wgk1": _pk(np.asarray(inp["L0_attn_Wgk1"], np.float32)),
        "Wgk2": np.ascontiguousarray(inp["L0_attn_Wgk2"], np.float32),
        "bgk2c": bgk2c,
        "Wo": np.ascontiguousarray(inp["L0_attn_Wo"], np.float32),
        "rms_c": rms_c,
        "gateW": _pk(np.asarray(inp["L0_ffn_gate_W"], np.float32)),
        "gatebr": np.asarray(inp["L0_ffn_gate_b"], np.float32).reshape(1, NE),
        "W1": np.ascontiguousarray(inp["L0_ffn_W1"], np.float32),
        "W2": np.ascontiguousarray(inp["L0_ffn_W2"], np.float32),
        "b1r": np.ascontiguousarray(inp["L0_ffn_b1"], np.float32),
        "b2r": np.ascontiguousarray(inp["L0_ffn_b2"], np.float32),
        "be2r": np.ascontiguousarray(inp["L0_ffn_be2"], np.float32),
        "g1b": bc(g1), "be1b": bc(be1), "g2b": bc(g2),
        "identf": identf, "identb": identb, "maskU": maskU,
        "onesrow": onesrow, "onescan": onescan, "cvals": cvals,
    }
    if hh == 0:
        im["Wk_p"] = z((D, DK), np.float32)
        im["Wv_p"] = z((D, DV), np.float32)
        im["Wgk1_p"] = z((128, NDC * LR), np.float32)
        im["Wgk2_p"] = z((LR, DK), np.float32)
        im["bgk2c_p"] = z((128, NKC), np.float32)
    else:
        im["Wk_p"] = im["Wk"]
        im["Wv_p"] = im["Wv"]
        im["Wgk1_p"] = im["Wgk1"]
        im["Wgk2_p"] = im["Wgk2"]
        im["bgk2c_p"] = bgk2c
    return im


def kernel(**inp):
    cfg = Cfg()
    nc = build(cfg)
    in_maps = [make_inputs(c, cfg, inp) for c in range(8)]
    res = run_bass_kernel_spmd(nc, in_maps, core_ids=list(range(8)))
    odt = np.asarray(res.results[0]["out"]).dtype
    out = np.empty((4, 4096, D), odt)
    for c in range(8):
        b, hh = c // 2, c % 2
        out[b, hh * cfg.T:(hh + 1) * cfg.T, :] = res.results[c]["out"]
    return out


# version 14
# speedup vs baseline: 1.0020x; 1.0020x over previous
import numpy as np
import concourse.bass as bass
import concourse.mybir as mybir
from concourse.bass_utils import run_bass_kernel_spmd

F32 = mybir.dt.float32
BF16 = mybir.dt.bfloat16
ALU = mybir.AluOpType
ACT = mybir.ActivationFunctionType
AXL = mybir.AxisListType

D = 1024
DK = 1024
DV = 2048
H = 4
HDK = DK // H
HDV = DV // H
NE = 4
LR = 16
NDC = D // 128
NKC = DK // 128
NVC = DV // 128
CH = 128
GN = 16.0
LN16 = float(np.log(1.0 / GN))
EPS = 1e-5


class Cfg:
    def __init__(self, ST=1024, NST=2, NPST=2, debug=False, stop_after=None):
        self.ST = ST
        self.NST = NST
        self.NPST = NPST
        self.T = ST * NST
        self.P = ST * NPST
        self.NTC = ST // 128
        self.NBLK = ST // 512
        self.debug = debug
        self.stop_after = stop_after


class Sem:
    def __init__(self, h, name):
        self.h = h
        self.n = 0
        self.name = name


class Prog:
    ENGINES = ("sync", "scalar", "vector", "tensor", "gpsimd")

    def __init__(self):
        self.q = {e: [] for e in self.ENGINES}

    def emit(self, eng, fn, wait=(), inc=None, inck=1):
        wait = tuple((s, v) for (s, v) in wait if v > 0)
        for s, v in wait:
            assert v <= s.n, f"wait on future {s.name}: {v} > {s.n}"
        if inc is not None:
            inc.n += inck

        def run(e, fn=fn, wait=wait, inc=inc, inck=inck):
            for s, v in wait:
                e.wait_ge(s.h, v)
            r = fn(e)
            if inc is not None:
                r.then_inc(inc.h, inck)
        self.q[eng].append(run)


def _rearr(ap):
    return ap.rearrange("(c p) n -> p c n", p=128)


MM_TAGS = []
MM_BIR = {}


def build(cfg: Cfg):
    MM_TAGS.clear()
    MM_BIR.clear()
    nc = bass.Bass("TRN2", target_bir_lowering=False,
                   dynamic_dma_scratch_size=16384)
    ST, NTC, NBLK = cfg.ST, cfg.NTC, cfg.NBLK
    NCH = ST // CH

    dt = nc.dram_tensor
    x_tm = dt("x_tm", [cfg.T, D], F32, kind="ExternalInput").ap()
    xp_tm = dt("xp_tm", [cfg.P, D], F32, kind="ExternalInput").ap()
    Wq = dt("Wq", [D, DK], F32, kind="ExternalInput").ap()
    Wk = dt("Wk", [D, DK], F32, kind="ExternalInput").ap()
    Wv = dt("Wv", [D, DV], F32, kind="ExternalInput").ap()
    Wg = dt("Wg", [D, DV], F32, kind="ExternalInput").ap()
    Wgk1 = dt("Wgk1", [128, NDC * LR], F32, kind="ExternalInput").ap()
    Wgk2 = dt("Wgk2", [LR, DK], F32, kind="ExternalInput").ap()
    bgk2c = dt("bgk2c", [128, NKC], F32, kind="ExternalInput").ap()
    Wk_p = dt("Wk_p", [D, DK], F32, kind="ExternalInput").ap()
    Wv_p = dt("Wv_p", [D, DV], F32, kind="ExternalInput").ap()
    Wgk1_p = dt("Wgk1_p", [128, NDC * LR], F32, kind="ExternalInput").ap()
    Wgk2_p = dt("Wgk2_p", [LR, DK], F32, kind="ExternalInput").ap()
    bgk2c_p = dt("bgk2c_p", [128, NKC], F32, kind="ExternalInput").ap()
    Wo = dt("Wo", [DV, D], F32, kind="ExternalInput").ap()
    rms_c = dt("rms_c", [128, 4], F32, kind="ExternalInput").ap()
    gateW = dt("gateW", [128, NDC * NE], F32, kind="ExternalInput").ap()
    gatebr = dt("gatebr", [1, NE], F32, kind="ExternalInput").ap()
    W1 = dt("W1", [NE, D, D], F32, kind="ExternalInput").ap()
    W2 = dt("W2", [NE, D, D], F32, kind="ExternalInput").ap()
    b1r = dt("b1r", [NE, D], F32, kind="ExternalInput").ap()
    b2r = dt("b2r", [NE, D], F32, kind="ExternalInput").ap()
    be2r = dt("be2r", [NE, D], F32, kind="ExternalInput").ap()
    g1b = dt("g1b", [NE, 128, D], F32, kind="ExternalInput").ap()
    be1b = dt("be1b", [NE, 128, D], F32, kind="ExternalInput").ap()
    g2b = dt("g2b", [NE, 128, D], F32, kind="ExternalInput").ap()
    identf = dt("identf", [128, 128], F32, kind="ExternalInput").ap()
    identb = dt("identb", [128, 128], BF16, kind="ExternalInput").ap()
    maskU = dt("maskU", [128, 128], F32, kind="ExternalInput").ap()
    onesrow = dt("onesrow", [1, 512], BF16, kind="ExternalInput").ap()
    onescan = dt("onescan", [128, 128], F32, kind="ExternalInput").ap()
    cvals = dt("cvals", [128, 3], F32, kind="ExternalInput").ap()

    out_d = dt("out", [cfg.T, D], F32, kind="ExternalOutput").ap()
    dbg = {}
    if cfg.debug:
        def dout(nm, shape, dtp):
            dbg[nm] = dt("dbg_" + nm, shape, dtp, kind="ExternalOutput").ap()
        dout("xn", [128, NDC, ST], BF16)
        dout("bb", [128, NKC, ST], F32)
        dout("eb", [128, NKC, NCH], F32)
        dout("w", [128, NTC * NE], F32)
        dout("qg", [128, NKC, ST], BF16)
        dout("kd", [128, NKC, ST], BF16)
        dout("kgT", [128, NCH, NKC, 128], BF16)
        dout("v", [128, NTC, DV], BF16)
        dout("silu", [128, NTC, DV], BF16)
        dout("S", [128, NKC, HDV], BF16)
        dout("attn", [128, NTC, D], BF16)
        dout("cols", [128, 8, NTC], F32)

    TOP = 224 * 1024 - 64
    arena = nc.alloc_sbuf_tensor("arena", [128, 207 * 1024], mybir.dt.uint8)
    BASE = nc.lookup_mloc(arena).addr
    cur = [BASE]

    offs = {}

    def alloc(name, shape, dtype, at=None):
        nbytes = int(np.prod(shape[1:])) * (2 if dtype == BF16 else 4)
        if at is None:
            a = (cur[0] + 31) // 32 * 32
            cur[0] = a + nbytes
        else:
            a = at
        assert a + nbytes <= TOP, f"SBUF overflow at {name}: {a + nbytes}"
        offs[name] = a
        return nc.alloc_sbuf_tensor_at(name, list(shape), dtype, offset=a)

    s_xn = alloc("s_xn", [128, NDC, ST], BF16)
    s_qg = alloc("s_qg", [128, NKC, ST], BF16)
    s_kd = alloc("s_kd", [128, NKC, ST], BF16)
    s_kgT = alloc("s_kgT", [128, NCH, NKC, 128], BF16)
    s_S = alloc("s_S", [128, NKC, HDV], BF16)

    attn_sz = NTC * D * 2
    big_sz = max(NKC * ST * 4,
                 attn_sz + DV * 2 + NVC * 128 * 2 + 2048 + 4096)
    big0 = (cur[0] + 31) // 32 * 32
    cur[0] = big0 + big_sz
    s_bb = alloc("s_bb", [128, NKC, ST], F32, at=big0)
    s_Eq = alloc("s_Eq", [128, NKC, ST], BF16, at=big0)
    s_Em = alloc("s_Em", [128, NKC, ST], BF16, at=offs["s_kgT"])
    s_attn = alloc("s_attn", [128, NTC, D], BF16, at=big0)
    s_ogc = alloc("s_ogc", [128, DV], BF16, at=big0 + attn_sz)
    s_ogf = alloc("s_ogf", [128, NVC, 128], BF16, at=big0 + attn_sz + DV * 2)
    s_kgs = alloc("s_kgs", [128, 1024], BF16, at=big0 + attn_sz)
    jb_off = big0 + attn_sz + DV * 2 + NVC * 128 * 2
    s_jb = alloc("s_jb", [128, 1024], BF16, at=jb_off)
    s_gk1 = alloc("s_gk1", [16, ST], BF16, at=jb_off)
    s_rb = alloc("s_rb", [1, 2048], BF16, at=big0 + attn_sz)
    s_be2 = alloc("s_be2", [4, D], BF16, at=big0 + attn_sz + DV * 2)
    s_AT = alloc("s_AT", [128, H, 128], BF16, at=jb_off + 1024)
    s_rb2 = alloc("s_rb2", [1, 1024], BF16,
                  at=big0 + attn_sz + DV * 2 + 2048)
    s_diag = alloc("s_diag", [128, NKC, 128], BF16, at=jb_off + 4096)
    s_wgk2 = alloc("s_wgk2", [16, NKC, 128], BF16, at=jb_off + 2048)

    wslot = [alloc(f"s_w{i}", [128, 4096], BF16) for i in range(4)]

    v0 = (cur[0] + 31) // 32 * 32
    cur[0] = v0 + max(2 * NKC * ST * 2, NTC * DV * 2, 2 * NTC * D * 2)
    s_v = alloc("s_v", [128, NTC, DV], BF16, at=v0)
    s_z = alloc("s_z", [128, NTC, D], BF16, at=v0)
    s_h = alloc("s_h", [128, NTC, D], BF16, at=v0 + NTC * D * 2)

    hfm_sz = NDC * ST * 2
    si_sz = max(NTC * DV * 2, hfm_sz + 2 * D * 2,
                16384 + 8192 + 4096 + 4096)
    si0 = (cur[0] + 31) // 32 * 32
    cur[0] = si0 + si_sz
    s_silu = alloc("s_silu", [128, NTC, DV], BF16, at=si0)
    s_sg = alloc("s_sg", [128, NKC, ST], BF16, at=si0)
    s_xt = alloc("s_xt", [128, 2, 4, D], BF16, at=offs["s_qg"])
    s_xntm = alloc("s_xntm", [128, 2, 4, D], BF16, at=offs["s_kd"])
    s_w4 = alloc("s_w4", [128, NDC, 512], BF16, at=offs["s_qg"])
    s_w5 = alloc("s_w5", [128, NDC, 512], BF16, at=offs["s_kd"])
    s_hfm = alloc("s_hfm", [128, NDC, ST], BF16, at=si0)
    s_bc1 = alloc("s_bc1", [128, D], BF16, at=si0 + hfm_sz)
    s_bc2 = alloc("s_bc2", [128, D], BF16, at=si0 + hfm_sz + D * 2)
    s_gx = alloc("s_gx", [128, 2, D], F32, at=si0 + 16384)
    s_cf = alloc("s_cf", [128, 1024], F32, at=si0 + 16384 + 8192)
    s_jf = alloc("s_jf", [128, 1024], F32, at=si0 + 16384 + 8192 + 4096)

    s_eb = alloc("s_eb", [128, 2, NKC, NCH], F32)
    s_cols = alloc("s_cols", [128, 2, 8, NTC], F32)
    s_mcol = alloc("s_mcol", [128, 2, 4, NTC], F32)
    s_gsc = alloc("s_gsc", [128, 96], F32)
    s_wc = alloc("s_wc", [128, NTC, NE], F32)
    s_wfm = alloc("s_wfm", [4, NTC, 128], BF16)
    s_scol = alloc("s_scol", [128, 2, NTC, 2], F32)
    s_ssq = alloc("s_ssq", [128, 3, NTC], F32)
    s_wgk1 = alloc("s_wgk1", [128, NDC, LR], BF16)
    s_gateW = alloc("s_gateW", [128, NDC, NE], BF16)
    s_rms = alloc("s_rms", [128, 4], F32)
    s_bgk = alloc("s_bgk", [128, NKC], F32)
    c_idf = alloc("c_idf", [128, 128], F32)
    c_idb = alloc("c_idb", [128, 128], BF16)
    c_mU = alloc("c_mU", [128, 128], F32)
    c_os = alloc("c_os", [128, 128], F32)
    c_or1 = alloc("c_or1", [1, 128], BF16)
    s_gateb = alloc("s_gateb", [1, NE], BF16)
    s_cv = alloc("s_cv", [128, 3], F32)
    assert cur[0] <= BASE + 207 * 1024, f"SBUF arena overflow: {cur[0] - BASE}"
    print("SBUF used:", cur[0] - BASE)

    pa = nc.alloc_psum_tensor("pa", [128, 512], F32)
    pb = nc.alloc_psum_tensor("pb", [128, 512], F32)
    pc_ = nc.alloc_psum_tensor("pc", [128, 512], F32)
    pd = nc.alloc_psum_tensor("pd", [128, 512], F32)
    pf1 = nc.alloc_psum_tensor("pf1", [128, 1024], F32)
    pf2 = nc.alloc_psum_tensor("pf2", [128, 1024], F32)

    sem_ctx = []
    P = Prog()

    def mksem(name):
        cm = nc.semaphore(name)
        h = cm.__enter__()
        sem_ctx.append(cm)
        return Sem(h, name)

    gd = mksem("gd")
    gx = mksem("gx")
    gbc = mksem("gbc")
    gbr = mksem("gbr")
    gsl = [mksem(f"gs{i}") for i in range(4)]
    tg = [mksem("tgA"), mksem("tgB")]
    gbr2 = [mksem("gbrA"), mksem("gbrB")]
    pl = mksem("pl")
    gs4 = mksem("gs4")
    gs5 = mksem("gs5")
    xd = mksem("xd")
    tp = mksem("tp")
    te = mksem("te")
    ve = mksem("ve")
    sc = mksem("sc")

    def slotv(i, c, n):
        return wslot[i][:, :].rearrange("p (c n) -> p c n", n=n)[:, 0:c, :]

    def g_op(fn, wait=()):
        P.emit("gpsimd", fn, wait=wait, inc=pl)

    def g_dma(dst_ap, src_ap, wait=(), sem=None):
        P.emit("gpsimd", lambda e, d=dst_ap, s=src_ap: e.dma_start(out=d, in_=s),
               wait=wait, inc=sem or gd, inck=16)

    def x_dma(dst_ap, src_ap, wait=()):
        P.emit("scalar", lambda e, d=dst_ap, s=src_ap: e.dma_start(out=d, in_=s),
               wait=wait, inc=xd, inck=16)

    def t_dma(dst_ap, src_ap, wait=()):
        P.emit("sync",
               lambda e, d=dst_ap, s=src_ap: e.dma_start(out=d, in_=s, transpose=True),
               wait=wait, inc=tp, inck=16)

    def mm(out, lhsT, rhs, start, stop, wait=(), inc=None):
        import sys as _sys
        tag = _sys._getframe(1).f_lineno
        wdesc = ",".join(f"{s.name}>={v}" for s, v in wait)

        def _do(e, o=out, l=lhsT, r=rhs, st=start, sp=stop, tag=tag, wd=wdesc):
            ret = e.matmul(out=o, lhsT=l, rhs=r, start=st, stop=sp)
            try:
                MM_BIR[ret.ins.name] = (tag, wd)
            except Exception:
                pass
            return ret
        P.emit("tensor", _do, wait=wait, inc=inc)

    def v_op(fn, wait=(), inc=None):
        P.emit("vector", fn, wait=tuple(wait) + ((ve, ve.n),), inc=inc or ve)

    def s_op(fn, wait=(), inc=None):
        P.emit("scalar", fn, wait=tuple(wait) + ((sc, sc.n),), inc=inc or sc)

    wu_i = [0]
    wu_freed = [None, None, None, None]

    def wu_next(dram_ap2d):
        i = wu_i[0] % 4
        wu_i[0] += 1
        r, ncol = dram_ap2d.shape
        w = [wu_freed[i]] if wu_freed[i] is not None else []
        g_dma(slotv(i, r // 128, ncol), _rearr(dram_ap2d), wait=w, sem=gsl[i])
        return i

    def wu_done(i):
        wu_freed[i] = (te, te.n)

    class Rot:
        def __init__(self, banks):
            self.banks = banks
            self.mark = [None] * len(banks)
            self.i = 0

        def next(self):
            b = self.i % len(self.banks)
            self.i += 1
            m = self.mark[b]
            return self.banks[b], ([m] if m is not None else [])

        def done(self, sem):
            b = (self.i - 1) % len(self.banks)
            self.mark[b] = (sem, sem.n)

    rot = Rot([pa, pb])
    rot2 = Rot([pc_, pd, pf2[:, 512:1024]])
    rot2p = Rot([pf1[:, 0:512], pf1[:, 512:1024], pf2[:, 0:512]])

    g_dma(c_idf[:, :], identf)
    g_dma(c_idb[:, :], identb)
    g_dma(c_mU[:, :], maskU)
    g_dma(c_os[:, :], onescan)
    g_dma(s_cv[:, :], cvals)
    g_dma(s_rms[:, :], rms_c)
    g_dma(c_or1[0:1, :], onesrow[0:1, 0:128])
    g_dma(s_gateb[:, :], gatebr)
    cm = gd.n
    v_op(lambda e: e.memset(s_S[:, :, :], 0.0), wait=[(gd, cm)])

    ones1 = c_or1[0:1, 0:128]

    def stage_A_blk(x_src, t0, blk, cpar, load_w, sq_w=()):
        par = blk % 2
        bsl = s_xt[:, par, :, :]
        xnb = s_xntm[:, par, :, :]
        rows = x_src[t0 + blk * 512: t0 + (blk + 1) * 512, :]
        g_dma(bsl[:, :, :], _rearr(rows), wait=[(gx, gx.n)] + list(load_w), sem=gx)
        gw = gx.n
        q = s_cols[:, cpar]
        for i in range(4):
            tc = blk * 4 + i
            s_op(lambda e, b=bsl, xn=xnb, i=i, tc=tc: e.activation(
                out=xn[:, i, :], in_=b[:, i, :], func=ACT.Square,
                accum_out=s_ssq[:, 2, tc:tc + 1]),
                wait=[(gx, gw), (gd, cm)] +
                     (([(tg[par], tg[par].n)] + list(sq_w)) if i == 0 else []))
            v_op(lambda e, b=bsl, q=q, i=i, tc=tc: e.reduce_sum(
                out=q[:, 0, tc:tc + 1], in_=b[:, i, :], axis=AXL.X),
                wait=[(gx, gw)])
        t = slice(blk * 4, blk * 4 + 4)
        v_op(lambda e, q=q, t=t: e.tensor_scalar(
            out=q[:, 1, t], in0=q[:, 0, t], scalar1=1.0 / D, scalar2=None,
            op0=ALU.mult), wait=[(ve, ve.n)])
        v_op(lambda e, q=q, t=t: e.tensor_tensor(
            out=q[:, 2, t], in0=q[:, 1, t], in1=q[:, 1, t], op=ALU.mult))
        v_op(lambda e, q=q, t=t: e.scalar_tensor_tensor(
            out=q[:, 3, t], in0=s_ssq[:, 2, t], scalar=1.0 / D, in1=q[:, 2, t],
            op0=ALU.mult, op1=ALU.subtract), wait=[(sc, sc.n)])
        s_op(lambda e, q=q, t=t: e.activation(out=q[:, 4, t], in_=q[:, 3, t],
             func=ACT.Ln, bias=s_cv[:, 0:1]), wait=[(ve, ve.n)])
        s_op(lambda e, q=q, t=t: e.activation(out=q[:, 5, t], in_=q[:, 4, t],
             func=ACT.Exp, scale=-0.5))
        v_op(lambda e, q=q, t=t: e.scalar_tensor_tensor(
            out=q[:, 6, t], in0=q[:, 1, t], scalar=-1.0, in1=q[:, 5, t],
            op0=ALU.mult, op1=ALU.mult), wait=[(sc, sc.n)])
        v_op(lambda e, q=q, t=t: e.tensor_scalar(
            out=q[:, 7, t], in0=q[:, 5, t], scalar1=1.0, scalar2=None,
            op0=ALU.add))
        for i in range(4):
            tc = blk * 4 + i
            v_op(lambda e, q=q, i=i, tc=tc, b=bsl, xn=xnb: e.tensor_scalar(
                out=xn[:, i, :], in0=b[:, i, :],
                scalar1=q[:, 5, tc:tc + 1], scalar2=q[:, 6, tc:tc + 1],
                op0=ALU.mult, op1=ALU.add),
                wait=[(ve, ve.n)])
        return (par, xnb, blk, (ve, ve.n))

    def stage_A_xpose(blks, xn_free):
        xnm = []
        first = [xn_free] if xn_free is not None else []
        for par, xnb, blk, am in blks:
            for i in range(4):
                tc = blk * 4 + i
                P.emit("sync",
                       lambda e, tc=tc, i=i, xn=xnb:
                       e.dma_start(out=s_xn[:, :, tc * 128:(tc + 1) * 128],
                                   in_=xn[:, i, :].rearrange("p (c n) -> p c n", n=128),
                                   transpose=True),
                       wait=[am] + first, inc=tg[par], inck=16)
                first = []
            xnm.append((tg[par], tg[par].n))
        return xnm

    def stage_A(x_src, t0, cpar):
        blks = [stage_A_blk(x_src, t0, blk, cpar, ()) for blk in range(NBLK)]
        return stage_A_xpose(blks, None)

    b_done = [None]
    g_marks = [[]]
    wu_pref = [None]
    gtm = [None]
    wfm_pending = [None]

    def load_B_smalls(smalls, gate):
        wgk1d, wgk2d, bgk2d = smalls
        pre = [(gd, gd.n), (te, te.n)]
        g_dma(s_wgk1[:, :, :],
              wgk1d.rearrange("p (c n) -> p c n", n=LR), wait=pre)
        g_dma(s_wgk2[:, :, :], wgk2d.rearrange("k (c n) -> k c n", n=128))
        g_dma(s_bgk[:, :], bgk2d)
        if gate:
            g_dma(s_gateW[:, :, :],
                  gateW.rearrange("p (c n) -> p c n", n=NE))
        return gd.n

    def stage_B(gw, gate, xnm, epar=0, stop=None, mid=None):
        if stop == "Bs":
            return None, None
        if gate:
            for tc in range(NTC):
                for dc in range(NDC):
                    mm(pf2[:, tc * 4:(tc + 1) * 4],
                       s_xn[:, dc, tc * 128:(tc + 1) * 128], s_gateW[:, dc, :],
                       start=(dc == 0), stop=False,
                       wait=([(gd, gw), xnm[tc // 4]] +
                             ([(sc, sc.n)] if tc == 0 else []))
                       if dc == 0 else [])
                mm(pf2[:, tc * 4:(tc + 1) * 4], ones1, s_gateb[:, :],
                   start=False, stop=True, inc=te if tc == NTC - 1 else None)
            gtm[0] = (te, te.n)
        for blk in range(NBLK):
            ps, w0 = rot.next()
            for dc in range(NDC):
                mm(ps[0:16, :], s_wgk1[:, dc, :],
                   s_xn[:, dc, blk * 512:(blk + 1) * 512],
                   start=(dc == 0), stop=(dc == NDC - 1),
                   wait=(w0 + [(gd, gw), xnm[blk]]) if dc == 0 else [],
                   inc=te if dc == NDC - 1 else None)
            v_op(lambda e, ps=ps, blk=blk: e.tensor_copy(
                out=s_gk1[:, blk * 512:(blk + 1) * 512], in_=ps[0:16, :]),
                wait=[(te, te.n)])
            rot.done(ve)
        if stop == "B0":
            return None, None
        for kc in range(NKC):
            for blk in range(NBLK):
                ps, w0 = rot.next()
                bsl = slice(blk * 512, (blk + 1) * 512)
                mm(ps[:, :], s_wgk2[:, kc, :], s_gk1[:, bsl], start=True,
                   stop=True,
                   wait=w0 + ([(ve, ve.n)] if kc == 0 and blk == 0 else []),
                   inc=te)
                s_op(lambda e, ps=ps, kc=kc, bsl=bsl: e.activation(
                    out=s_sg[:, kc, bsl], in_=ps[:, :], func=ACT.Sigmoid,
                    bias=s_bgk[:, kc:kc + 1]),
                    wait=[(te, te.n)] + ([(gd, gw)] if kc == 0 and blk == 0 else []))
                rot.done(sc)
        if gate:
            g = s_gsc
            st4 = NTC * 4
            v_op(lambda e: e.tensor_copy(out=g[:, 0:st4], in_=pf2[:, 0:st4]),
                 wait=[gtm[0]])

            def gsl_(i):
                return g[:, i:st4:4]
            v_op(lambda e: e.tensor_tensor(out=g[:, st4:st4 + NTC], in0=gsl_(0),
                 in1=gsl_(1), op=ALU.max), wait=[(ve, ve.n)])
            v_op(lambda e: e.tensor_tensor(out=g[:, st4 + 8:st4 + 8 + NTC], in0=gsl_(2),
                 in1=gsl_(3), op=ALU.max))
            v_op(lambda e: e.tensor_tensor(out=g[:, st4 + 16:st4 + 16 + NTC], in0=gsl_(0),
                 in1=gsl_(1), op=ALU.min))
            v_op(lambda e: e.tensor_tensor(out=g[:, st4 + 24:st4 + 24 + NTC], in0=gsl_(2),
                 in1=gsl_(3), op=ALU.min))
            v_op(lambda e: e.tensor_tensor(out=g[:, st4 + 32:st4 + 32 + NTC],
                 in0=g[:, st4:st4 + NTC], in1=g[:, st4 + 8:st4 + 8 + NTC], op=ALU.max))
            v_op(lambda e: e.tensor_tensor(out=g[:, st4 + 40:st4 + 40 + NTC],
                 in0=g[:, st4:st4 + NTC], in1=g[:, st4 + 8:st4 + 8 + NTC], op=ALU.min))
            v_op(lambda e: e.tensor_tensor(out=g[:, st4 + 48:st4 + 48 + NTC],
                 in0=g[:, st4 + 16:st4 + 16 + NTC], in1=g[:, st4 + 24:st4 + 24 + NTC],
                 op=ALU.max))
            v_op(lambda e: e.tensor_tensor(out=g[:, st4 + 56:st4 + 56 + NTC],
                 in0=g[:, st4 + 40:st4 + 40 + NTC], in1=g[:, st4 + 48:st4 + 48 + NTC],
                 op=ALU.max))
            for tc in range(NTC):
                v_op(lambda e, tc=tc: e.tensor_scalar(
                    out=g[:, tc * 4:(tc + 1) * 4], in0=g[:, tc * 4:(tc + 1) * 4],
                    scalar1=g[:, st4 + 32 + tc:st4 + 33 + tc], scalar2=None,
                    op0=ALU.subtract), wait=[(ve, ve.n)])
            v_op(lambda e: e.tensor_tensor(out=g[:, st4 + 40:st4 + 40 + NTC],
                 in0=g[:, st4 + 56:st4 + 56 + NTC], in1=g[:, st4 + 32:st4 + 32 + NTC],
                 op=ALU.subtract))
            s_op(lambda e: e.activation(out=g[:, 0:st4], in_=g[:, 0:st4],
                 func=ACT.Exp), wait=[(ve, ve.n)])
            s_op(lambda e: e.activation(out=g[:, st4 + 40:st4 + 40 + NTC],
                 in_=g[:, st4 + 40:st4 + 40 + NTC], func=ACT.Exp))
            for tc in range(NTC):
                v_op(lambda e, tc=tc: e.tensor_scalar(
                    out=s_wc[:, tc, :], in0=g[:, tc * 4:(tc + 1) * 4],
                    scalar1=g[:, st4 + 40 + tc:st4 + 41 + tc], scalar2=None,
                    op0=ALU.is_ge), wait=[(sc, sc.n)])
            v_op(lambda e: e.tensor_tensor(
                out=g[:, 0:st4], in0=g[:, 0:st4],
                in1=s_wc[:, :, :].rearrange("p a b -> p (a b)"), op=ALU.mult),
                wait=[(ve, ve.n)])
            v_op(lambda e: e.reduce_sum(
                out=g[:, st4 + 48:st4 + 48 + NTC],
                in_=g[:, 0:st4].rearrange("p (a b) -> p a b", b=4), axis=AXL.X),
                wait=[(ve, ve.n)])
            v_op(lambda e: e.reciprocal(out=g[:, st4 + 48:st4 + 48 + NTC],
                 in_=g[:, st4 + 48:st4 + 48 + NTC]))
            for tc in range(NTC):
                v_op(lambda e, tc=tc: e.tensor_scalar(
                    out=s_wc[:, tc, :], in0=g[:, tc * 4:(tc + 1) * 4],
                    scalar1=g[:, st4 + 48 + tc:st4 + 49 + tc], scalar2=None,
                    op0=ALU.mult), wait=[(ve, ve.n)])
            wfm_pending[0] = (ve, ve.n)
        if mid is not None:
            mid()
        eqms = []
        emms = []
        for hf in range(2):
            s_op(lambda e, hf=hf: e.activation(
                out=s_bb[:, hf * 4:(hf + 1) * 4, :], in_=s_sg[:, hf * 4:(hf + 1) * 4, :],
                func=ACT.Ln),
                wait=[(xd, xd.n), (ve, ve.n)] if hf == 0 else [])
            lnm = sc.n
            for kc in range(hf * 4, (hf + 1) * 4):
                for c in range(NCH):
                    v_op(lambda e, kc=kc, c=c: e.tensor_tensor_scan(
                        out=s_bb[:, kc, c * 128:(c + 1) * 128],
                        data0=c_os[:, :], data1=s_bb[:, kc, c * 128:(c + 1) * 128],
                        initial=0.0, op0=ALU.mult, op1=ALU.add),
                        wait=[(sc, lnm)] if (kc == hf * 4 and c == 0) else [])
            bbok_h = ve.n
            s_op(lambda e, hf=hf: e.activation(
                out=s_Em[:, hf * 4:(hf + 1) * 4, :], in_=s_bb[:, hf * 4:(hf + 1) * 4, :],
                func=ACT.Exp, scale=-1.0 / GN),
                wait=[(ve, bbok_h)] + ([(te, te.n), (tp, tp.n)] if hf == 0 else []))
            emms.append((sc, sc.n))
            s_op(lambda e, hf=hf: e.activation(
                out=s_eb[:, epar, hf * 4:(hf + 1) * 4, :],
                in_=s_bb[:, hf * 4:(hf + 1) * 4, 127::128],
                func=ACT.Exp, scale=1.0 / GN))
        for hf in range(2):
            s_op(lambda e, hf=hf: e.activation(
                out=s_Eq[:, hf * 4:(hf + 1) * 4, :], in_=s_bb[:, hf * 4:(hf + 1) * 4, :],
                func=ACT.Exp, scale=1.0 / GN))
            eqms.append((sc, sc.n))
        b_done[0] = (te, te.n)
        return eqms, emms

    def proj_fm(dst, Wd, gate_sb, eq_marks, jb_ok=None, evscale=None):
        first = True
        firstjb = [jb_ok] if jb_ok else []
        jbtt = [None, None]
        halfused = [False, False]
        slots = []
        gws = []
        for half in range(2):
            slots.append(wu_next(Wd[:, half * 512:(half + 1) * 512]))
            gws.append(gsl[slots[half]].n)
        bmarks = []
        for blk in range(NBLK):
            bsl = slice(blk * 512, (blk + 1) * 512)
            for kc in range(NKC):
                half, k2 = kc // 4, kc % 4
                slot = slots[half]
                ps, w0 = rot.next()
                for dc in range(NDC):
                    mm(ps[:, :],
                       slotv(slot, NDC, 512)[:, dc, k2 * 128:(k2 + 1) * 128],
                       s_xn[:, dc, bsl], start=(dc == 0), stop=(dc == NDC - 1),
                       wait=(w0 + [(gsl[slot], gws[half])] +
                             ([(ve, ve.n)] if first else [])) if dc == 0 else [],
                       inc=te if dc == NDC - 1 else None)
                first = False
                hj = (rot.i - 1) % 2
                jb = s_jb[:, hj * 512:(hj + 1) * 512]
                ew = []
                if not halfused[half]:
                    ew = [eq_marks[half]]
                    halfused[half] = True
                s_op(lambda e, ps=ps, jb=jb: e.activation(
                    out=jb, in_=ps[:, :], func=ACT.Identity,
                    scale=1.0 if evscale is None else evscale),
                    wait=[(te, te.n)] + ew + firstjb +
                         ([jbtt[hj]] if jbtt[hj] is not None else []))
                firstjb = []
                rot.done(sc)
                v_op(lambda e, jb=jb, kc=kc, bsl=bsl: e.tensor_tensor(
                    out=dst[:, kc, bsl], in0=jb, in1=gate_sb[:, kc, bsl],
                    op=ALU.mult), wait=[(sc, sc.n)])
                jbtt[hj] = (ve, ve.n)
            bmarks.append((ve, ve.n))
        wu_done(slots[0])
        wu_done(slots[1])
        return bmarks

    def proj_tm(dst, Wd, act, vm, pre_slot=None, ev_w=(), ded=None):
        first = True
        evw = list(ev_w)
        for q4 in range(4):
            if q4 == 0 and ded is not None:
                wview, gwait = ded
                slot = None
            else:
                slot = pre_slot if (q4 == 0 and pre_slot is not None) \
                    else wu_next(Wd[:, q4 * 512:(q4 + 1) * 512])
                wview = slotv(slot, NDC, 512)
                gwait = (gsl[slot], gsl[slot].n)
            for tc in range(NTC):
                ps, w0 = rot.next()
                for dc in range(NDC):
                    mm(ps[:, :], s_xn[:, dc, tc * 128:(tc + 1) * 128],
                       wview[:, dc, :],
                       start=(dc == 0), stop=(dc == NDC - 1),
                       wait=(w0 + [gwait] +
                             ([vm] if first else [])) if dc == 0 else [],
                       inc=te if dc == NDC - 1 else None)
                first = False
                if act is None:
                    v_op(lambda e, ps=ps, tc=tc, q4=q4: e.tensor_copy(
                        out=dst[:, tc, q4 * 512:(q4 + 1) * 512], in_=ps[:, :]),
                        wait=[(te, te.n)])
                    rot.done(ve)
                else:
                    s_op(lambda e, ps=ps, tc=tc, q4=q4: e.activation(
                        out=dst[:, tc, q4 * 512:(q4 + 1) * 512], in_=ps[:, :],
                        func=act), wait=[(te, te.n)] + evw)
                    evw = []
                    rot.done(sc)
            if slot is not None:
                wu_done(slot)

    def stage_C(prefix, eq_marks, em_marks, pre=(None, None),
                ded=(None, None), epar=0):
        vm = (te, te.n)
        proj_tm(s_v, Wv_p if prefix else Wv, None, vm, pre_slot=pre[0],
                ded=ded[0])
        if not prefix:
            proj_tm(s_silu, Wg, ACT.Silu, vm, pre_slot=pre[1],
                    ev_w=g_marks[0], ded=ded[1])
        qbm = None
        if not prefix:
            qbm = proj_fm(s_qg, Wq, s_Eq, eq_marks, jb_ok=eq_marks[1],
                          evscale=1.0 / 16.0)
        kbm = proj_fm(s_kd, Wk_p if prefix else Wk, s_Em, em_marks,
                      jb_ok=None if not prefix else eq_marks[1])
        tpms = []
        for kc in range(NKC):
            prev_tp = tp.n
            for c in range(NCH):
                v_op(lambda e, kc=kc, c=c: e.tensor_scalar(
                    out=s_kgs[:, c * 128:(c + 1) * 128],
                    in0=s_kd[:, kc, c * 128:(c + 1) * 128],
                    scalar1=s_eb[:, epar, kc, c:c + 1], scalar2=None,
                    op0=ALU.mult),
                    wait=[(ve, ve.n), (tp, prev_tp)] +
                         ([eq_marks[1], (tg[0], tg[0].n), (tg[1], tg[1].n)]
                          if (kc == 0 and c == 0) else []))
            t_dma(s_kgT[:, :, kc, :],
                  s_kgs[:, 0:NCH * 128].rearrange("p (c n) -> p c n", n=128),
                  wait=[(ve, ve.n), (te, te.n), (sc, sc.n)])
            tpms.append((tp, tp.n))
        return tpms, qbm, kbm

    def stage_DE(prefix, tpms, qbm=None, kbm=None, epar=0):
        tp0 = tp.n
        xd0 = xd.n
        ebm = sc.n
        rms_done = [None]
        if not prefix:
            slots = []
            gws = []
            for q in range(4):
                slots.append(wu_next(Wo[q * 512:(q + 1) * 512, :]))
                gws.append(gsl[slots[q]].n)
            for q in range(4):
                for vci in range(4):
                    s_op(lambda e, q=q, vci=vci: e.activation(
                        out=slotv(slots[q], 4, 1024)[:, vci, :],
                        in_=slotv(slots[q], 4, 1024)[:, vci, :],
                        func=ACT.Identity,
                        scale=s_rms[:, (q * 4 + vci) % 4:(q * 4 + vci) % 4 + 1]),
                        wait=[(gsl[slots[q]], gws[q])])
            rms_done[0] = sc.n
        v_done = ve.n
        sil_done = sc.n
        c_done = [(ve, ve.n), (sc, sc.n)]
        f2m = None
        f1m = None
        scpy = None
        dg_te = [None] * NKC

        def diag_build(c):
            ms = []
            for r in range(NKC):
                w0 = [] if dg_te[r] is None else [dg_te[r]]
                if c == 0 and r == 0:
                    w0 = w0 + [(sc, ebm)]
                v_op(lambda e, r=r, c=c: e.tensor_scalar(
                    out=s_diag[:, r, :], in0=c_idb[:, :],
                    scalar1=s_eb[:, epar, r, c:c + 1],
                    scalar2=None, op0=ALU.mult), wait=w0)
                ms.append((ve, ve.n))
            return ms

        dgv = diag_build(0)
        for c in range(NCH):
            if not prefix:
                for h in range(H):
                    for k in range(2):
                        kc = 2 * h + k
                        w0 = []
                        if h == 0 and k == 0:
                            w0 = [f2m] if c > 0 else [wfm_pending[0], (sc, ebm)]
                            if c % 4 == 0:
                                w0 = w0 + [qbm[c // 4], kbm[c // 4]]
                            if c == 0:
                                w0 = [x for x in w0 if x is not None]
                        mm(pf2[:, h * 128:(h + 1) * 128],
                           s_kd[:, kc, c * 128:(c + 1) * 128],
                           s_qg[:, kc, c * 128:(c + 1) * 128],
                           start=(k == 0), stop=(k == 1), wait=w0,
                           inc=te if k == 1 else None)
                atm = te.n
                for h in range(H):
                    v_op(lambda e, h=h: e.tensor_tensor(
                        out=s_AT[:, h, :], in0=pf2[:, h * 128:(h + 1) * 128],
                        in1=c_mU[:, :], op=ALU.mult),
                        wait=[(te, atm)])
                f2m = (ve, ve.n)
                atvm = ve.n
                cpym = []
                for h in range(H):
                    po, w0 = rot.next()
                    w0 = w0 + [(ve, atvm)]
                    if c > 0 and h == 0 and scpy is not None:
                        w0.append(scpy)
                    mm(po[:, :], s_AT[:, h, :], s_v[:, c, h * 512:(h + 1) * 512],
                       start=True, stop=False, wait=w0)
                    for k in range(2):
                        kc = 2 * h + k
                        mm(po[:, :], s_qg[:, kc, c * 128:(c + 1) * 128],
                           s_S[:, kc, :], start=False, stop=(k == 1),
                           inc=te if k == 1 else None)
                    teo = te.n
                    v_op(lambda e, po=po, h=h: e.tensor_copy(
                        out=s_ogc[:, h * 512:(h + 1) * 512], in_=po[:, :]),
                        wait=[(te, teo)] + ([tpms[-1]] if (c == 0 and h == 0)
                                            else []))
                    rot.done(ve)
                    cpym.append(ve.n)
                    s_op(lambda e, h=h: e.activation(
                        out=s_jb[:, 0:512], in_=s_ogc[:, h * 512:(h + 1) * 512],
                        func=ACT.Square, accum_out=s_gsc[:, 80 + h:81 + h]),
                        wait=[(ve, cpym[h])] + ([(xd, xd0)] if (c == 0 and h == 0)
                                                else []))
            def state_rows(c=c):
                nonlocal scpy, dgv
                for r in range(NKC):
                    rr = rot2p if (prefix and r % 2 == 1) else rot2
                    pS, w2 = rr.next()
                    mm(pS[:, :], s_diag[:, r, :], s_S[:, r, :], start=True,
                       stop=False, wait=w2 + [dgv[r]])
                    mm(pS[:, :], s_kgT[:, c, r, :],
                       s_v[:, c, (r // 2) * 512:(r // 2 + 1) * 512],
                       start=False, stop=True,
                       wait=[tpms[r]] if c == 0 else [], inc=te)
                    dg_te[r] = (te, te.n)
                    s_op(lambda e, pS=pS, r=r: e.activation(
                        out=s_S[:, r, :], in_=pS[:, :], func=ACT.Identity),
                        wait=[dg_te[r]])
                    rr.done(sc)
                scpy = (sc, sc.n)
                if c + 1 < NCH:
                    dgv = diag_build(c + 1)
            if prefix:
                state_rows()
            if not prefix:
                s_op(lambda e: e.activation(
                    out=s_gsc[:, 84:88], in_=s_gsc[:, 80:84],
                    func=ACT.Ln, scale=1.0 / HDV, bias=s_cv[:, 0:1]))
                s_op(lambda e: e.activation(
                    out=s_gsc[:, 84:88], in_=s_gsc[:, 84:88],
                    func=ACT.Exp, scale=-0.5))
                scm = sc.n
                tph2 = []
                for h in range(H):
                    v_op(lambda e, h=h: e.tensor_scalar(
                        out=s_ogc[:, h * 512:(h + 1) * 512],
                        in0=s_ogc[:, h * 512:(h + 1) * 512],
                        scalar1=s_gsc[:, 84 + h:85 + h], scalar2=None,
                        op0=ALU.mult), wait=[(sc, scm)] if h == 0 else [])
                    v_op(lambda e, h=h, c=c: e.tensor_tensor(
                        out=s_ogc[:, h * 512:(h + 1) * 512],
                        in0=s_ogc[:, h * 512:(h + 1) * 512],
                        in1=s_silu[:, c, h * 512:(h + 1) * 512], op=ALU.mult))
                    if h == 1 or h == 3:
                        half = h // 2
                        P.emit("sync",
                               lambda e, half=half:
                               e.dma_start(out=s_ogf[:, half * 8:(half + 1) * 8, :],
                                           in_=s_ogc[:, half * 1024:(half + 1) * 1024]
                                           .rearrange("p (c n) -> p c n", n=128),
                                           transpose=True),
                               wait=[(ve, ve.n)] +
                                    ([wom] if c > 0 else
                                     ([(te, te.n)] if half == 0 else [])),
                               inc=tg[half], inck=16)
                        tph2.append((tg[half], tg[half].n))
                for q in range(4):
                    for vci in range(4):
                        vc = q * 4 + vci
                        for dmh in range(2):
                            w0 = []
                            if vc == 0 and dmh == 0:
                                w0 = [tph2[0]]
                                if c == 0:
                                    w0.append((sc, rms_done[0]))
                                if f1m is not None:
                                    w0.append(f1m)
                            elif vc == 8 and dmh == 0:
                                w0 = [tph2[1]]
                            mm(pf1[:, dmh * 512:(dmh + 1) * 512], s_ogf[:, vc, :],
                               slotv(slots[q], 4, 1024)[:, vci,
                                                        dmh * 512:(dmh + 1) * 512],
                               start=(vc == 0), stop=(vc == 15), wait=w0,
                               inc=te if (vc == 15 and dmh == 1) else None)
                wom = (te, te.n)
                v_op(lambda e, c=c: e.tensor_copy(out=s_attn[:, c, :],
                     in_=pf1[:, :]), wait=[wom])
                f1m = (ve, ve.n)
                state_rows()
        if not prefix:
            for q in range(4):
                wu_done(slots[q])

    def ln_cols(par, wcol, prev_colread):
        q = s_mcol[:, par, :, :]
        v_op(lambda e, q=q, par=par: e.tensor_tensor(
            out=q[:, 0, :], in0=s_scol[:, par, :, 0], in1=s_scol[:, par, :, 1],
            op=ALU.add), wait=[(sc, sc.n)])
        scolread = (ve, ve.n)
        v_op(lambda e, q=q: e.tensor_scalar(out=q[:, 0, :], in0=q[:, 0, :],
             scalar1=1.0 / D, scalar2=None, op0=ALU.mult))
        v_op(lambda e, q=q: e.tensor_tensor(out=q[:, 1, :], in0=q[:, 0, :],
             in1=q[:, 0, :], op=ALU.mult))
        v_op(lambda e, q=q, par=par: e.scalar_tensor_tensor(
            out=q[:, 1, :], in0=s_ssq[:, par, 0:NTC], scalar=1.0 / D,
            in1=q[:, 1, :], op0=ALU.mult, op1=ALU.subtract))
        ssqread = (ve, ve.n)
        s_op(lambda e, q=q: e.activation(out=q[:, 2, :], in_=q[:, 1, :],
             func=ACT.Ln, bias=s_cv[:, 0:1]), wait=[(ve, ve.n)])
        s_op(lambda e, q=q: e.activation(out=q[:, 2, :], in_=q[:, 2, :],
             func=ACT.Exp, scale=-0.5))
        if wcol is not None:
            for tc in range(NTC):
                v_op(lambda e, q=q, tc=tc: e.tensor_scalar(
                    out=q[:, 2, tc:tc + 1], in0=q[:, 2, tc:tc + 1],
                    scalar1=s_wc[:, tc, wcol:wcol + 1], scalar2=None,
                    op0=ALU.mult), wait=[(sc, sc.n)])
        v_op(lambda e, q=q: e.scalar_tensor_tensor(
            out=q[:, 3, :], in0=q[:, 0, :], scalar=-1.0, in1=q[:, 2, :],
            op0=ALU.mult, op1=ALU.mult), wait=[(sc, sc.n), (ve, ve.n)])
        return scolread, ssqread, (ve, ve.n)

    def z_phase(srcv, dstv, slot_a, slot_b, brow, par, evac_gate, first_w,
                sq_gate, colm):
        gwa, gwb = gsl[slot_a].n, gsl[slot_b].n
        evms = []
        for tc in range(NTC):
            for half in range(2):
                slot = slot_a if half == 0 else slot_b
                gw = gwa if half == 0 else gwb
                ps, w0 = rot.next()
                for dc in range(NDC):
                    w1_ = []
                    if dc == 0:
                        w1_ = w0 + [(gsl[slot], gw)]
                        if tc == 0 and half == 0:
                            w1_ += first_w
                    mm(ps[:, :], srcv[:, dc, tc * 128:(tc + 1) * 128],
                       slotv(slot, NDC, 512)[:, dc, :],
                       start=(dc == 0), stop=False, wait=w1_)
                mm(ps[:, :], ones1, brow[0:1, half * 512:(half + 1) * 512],
                   start=False, stop=True, inc=te)
                w2 = [(te, te.n)]
                if evac_gate is not None:
                    w2.append(evac_gate[tc])
                if tc == 0 and half == 0 and colm is not None:
                    w2.append(colm)
                s_op(lambda e, ps=ps, tc=tc, half=half: e.activation(
                    out=dstv[:, tc, half * 512:(half + 1) * 512], in_=ps[:, :],
                    func=ACT.Identity,
                    accum_out=s_scol[:, par, tc, half:half + 1]), wait=w2)
                rot.done(sc)
            w3 = [(sc, sc.n)]
            if tc == 0 and sq_gate is not None:
                w3.append(sq_gate)
            s_op(lambda e, tc=tc: e.activation(out=s_jb[:, :], in_=dstv[:, tc, :],
                 func=ACT.Square, accum_out=s_ssq[:, par, tc:tc + 1]), wait=w3)
            evms.append((sc, sc.n))
        wu_done(slot_a)
        wu_done(slot_b)
        return evms, (te, te.n)

    def apply1(e, zv, hv, par, relu_gate, p2gate, p3gate):
        marks = []
        p2m = p3m = None
        for tc in range(NTC):
            w1_ = [(ve, ve.n)]
            if tc == 0:
                w1_.append((gbc, gbc.n))
            v_op(lambda e_, tc=tc, zv=zv: e_.tensor_scalar(
                out=zv[:, tc, :], in0=zv[:, tc, :],
                scalar1=s_mcol[:, par, 2, tc:tc + 1],
                scalar2=s_mcol[:, par, 3, tc:tc + 1],
                op0=ALU.mult, op1=ALU.add), wait=w1_)
            w2_ = [p2gate] if (tc == 0 and p2gate is not None) else []
            v_op(lambda e_, tc=tc, zv=zv: e_.tensor_tensor(
                out=zv[:, tc, :], in0=zv[:, tc, :], in1=s_bc1[:, :],
                op=ALU.mult), wait=w2_)
            if tc == NTC - 1:
                p2m = (ve, ve.n)
            w3_ = [p3gate] if (tc == 0 and p3gate is not None) else []
            v_op(lambda e_, tc=tc, zv=zv: e_.tensor_tensor(
                out=zv[:, tc, :], in0=zv[:, tc, :], in1=s_bc2[:, :],
                op=ALU.add), wait=w3_)
            if tc == NTC - 1:
                p3m = (ve, ve.n)
            w4_ = [relu_gate] if (tc == 0 and relu_gate is not None) else []
            v_op(lambda e_, tc=tc, zv=zv, hv=hv: e_.tensor_scalar(
                out=hv[:, tc, :], in0=zv[:, tc, :], scalar1=0.0, scalar2=None,
                op0=ALU.max), wait=w4_)
            marks.append((ve, ve.n))
        return marks, p2m, p3m

    def apply2_mix(e, zv, par, p2gate):
        marks = []
        p2m = None
        for tc in range(NTC):
            w1_ = [(ve, ve.n)]
            if tc == 0:
                w1_.append((gbc, gbc.n))
            v_op(lambda e_, tc=tc, zv=zv: e_.tensor_scalar(
                out=zv[:, tc, :], in0=zv[:, tc, :],
                scalar1=s_mcol[:, par, 2, tc:tc + 1],
                scalar2=s_mcol[:, par, 3, tc:tc + 1],
                op0=ALU.mult, op1=ALU.add), wait=w1_)
            w2_ = [p2gate] if (tc == 0 and p2gate is not None) else []
            v_op(lambda e_, tc=tc, zv=zv: e_.tensor_tensor(
                out=zv[:, tc, :], in0=zv[:, tc, :], in1=s_bc1[:, :],
                op=ALU.mult), wait=w2_)
            if tc == NTC - 1:
                p2m = (ve, ve.n)
            v_op(lambda e_, tc=tc, zv=zv: e_.tensor_tensor(
                out=s_attn[:, tc, :], in0=s_attn[:, tc, :], in1=zv[:, tc, :],
                op=ALU.add))
            marks.append((ve, ve.n))
        return marks, p2m

    def stage_F(side=None):
        for tc in range(NTC):
            P.emit("tensor", lambda e, tc=tc: e.transpose(
                out=pf1[0:4, 0:128], in_=s_wc[:, tc, :], identity=c_idf[:, :]),
                wait=([wfm_pending[0]] if tc == 0 else []) + [(ve, ve.n)], inc=te)
            v_op(lambda e, tc=tc: e.tensor_copy(out=s_wfm[0:4, tc, :],
                 in_=pf1[0:4, 0:128]), wait=[(te, te.n)])
        g_dma(s_be2[:, :], be2r[:, :], wait=[(te, te.n), (gd, gd.n)])
        be2m = gd.n
        for tc in range(NTC):
            for half in range(2):
                ps, w0 = rot.next()
                mm(ps[:, :], s_wfm[0:4, tc, :],
                   s_be2[0:4, half * 512:(half + 1) * 512],
                   start=True, stop=True,
                   wait=w0 + ([(gd, be2m)]
                              if tc == 0 and half == 0 else []), inc=te)
                v_op(lambda e, ps=ps: e.tensor_copy(out=s_jb[:, 0:512],
                     in_=ps[:, :]),
                     wait=[(te, te.n)] + ([(sc, sc.n)] if tc == 0 and half == 0
                                          else []))
                rot.done(ve)
                v_op(lambda e, tc=tc, half=half: e.tensor_tensor(
                    out=s_attn[:, tc, half * 512:(half + 1) * 512],
                    in0=s_attn[:, tc, half * 512:(half + 1) * 512],
                    in1=s_jb[:, 0:512], op=ALU.add))
        s_k3 = s_kgT[:, :, :, :].rearrange("p a b n -> p a (b n)")
        jbm = (ve, ve.n)
        phases = [("z1", 0), ("z1", 1), ("z2", 0), ("z1", 2), ("z2", 1),
                  ("z1", 3), ("z2", 2), ("z2", 3)]
        F0ev = [(te, te.n)]
        st_ = {e: {} for e in range(NE)}
        last_colread = [None, None]
        last_ssqread = [jbm, jbm]
        last_p2 = [None]
        last_p3 = [None]

        prep = {}

        def prepare(pi):
            if pi >= len(phases) or pi in prep:
                return
            kind, e = phases[pi]
            par = pi % 2
            if kind == "z1":
                ra = s_rb[0:1, 0:1024] if par == 0 else s_rb2[0:1, :]
                g_dma(ra, b1r[e:e + 1, :], sem=gbr2[par], wait=[(te, te.n)])
                sa = wu_next(W1[e, :, 0:512])
                sb = wu_next(W1[e, :, 512:1024])
            else:
                ra = (s_rb[0:1, 1024:2048] if par == 0 else
                      s_diag[0:1, :, :].rearrange("p a b -> p (a b)")[0:1, 0:1024])
                g_dma(ra, b2r[e:e + 1, :], sem=gbr2[par], wait=[(te, te.n)])
                sa = wu_next(W2[e, :, 0:512])
                sb = wu_next(W2[e, :, 512:1024])
            prep[pi] = (ra, (gbr2[par], gbr2[par].n), sa, sb)

        def emit_phase(pi):
            kind, e = phases[pi]
            par = pi % 2
            ra, gbm, sa, sb = prep[pi]
            if kind == "z1":
                egate = None if e == 0 else st_[e - 1].get("apply1")
                first_w = [gbm]
                colm0 = last_colread[par]
                if pi == 0:
                    colm0 = F0ev[0]
                evms, tem = z_phase(s_xn, s_z, sa, sb, ra, par, egate, first_w,
                                    last_ssqread[par], colm0)
                st_[e]["z1ev"] = evms
                st_[e]["z1te"] = tem
                st_[e]["par1"] = par
            else:
                egate = None if e == 0 else st_[e - 1].get("mix")
                first_w = [gbm, st_[e]["tph"]]
                evms, tem = z_phase(s_hfm, s_k3, sa, sb, ra, par, egate, first_w,
                                    last_ssqread[par], last_colread[par])
                st_[e]["z2ev"] = evms
                st_[e]["z2te"] = tem
                st_[e]["par2"] = par

        def emit_tail(kind, e):
            if kind == "z1":
                par = st_[e]["par1"]
                g_dma(s_bc1[:, :], g1b[e, :, :], sem=gbc,
                      wait=[last_p2[0]] if last_p2[0] else [(ve, ve.n)])
                g_dma(s_bc2[:, :], be1b[e, :, :], sem=gbc,
                      wait=[last_p3[0]] if last_p3[0] else [])
                cr, sr, am = ln_cols(par, None, None)
                last_colread[par], last_ssqread[par] = cr, sr
                relu_gate = None if e == 0 else st_[e - 1]["z2te"]
                marks, p2m, p3m = apply1(e, s_z, s_h, par, relu_gate, None, None)
                st_[e]["apply1"] = marks
                last_p2[0] = p2m
                last_p3[0] = p3m
                for tc in range(NTC):
                    t_dma(s_hfm[:, :, tc * 128:(tc + 1) * 128],
                          s_h[:, tc, :].rearrange("p (c n) -> p c n", n=128),
                          wait=[marks[tc]] + ([st_[e - 1]["z2te"]] if (e > 0 and tc == 0)
                                              else ([(te, te.n)] if tc == 0 else [])))
                st_[e]["tph"] = (tp, tp.n)
            else:
                par = st_[e]["par2"]
                g_dma(s_bc1[:, :], g2b[e, :, :], sem=gbc,
                      wait=[last_p2[0]] if last_p2[0] else [(ve, ve.n)])
                cr, sr, am = ln_cols(par, e, None)
                last_colread[par], last_ssqread[par] = cr, sr
                marks, p2m = apply2_mix(e, s_k3, par, None)
                st_[e]["mix"] = marks
                last_p2[0] = p2m

        for pi, (kind, e) in enumerate(phases):
            prepare(pi)
            prepare(pi + 1)
            emit_phase(pi)
            if kind == "z1" and e == 0:
                emit_tail("z1", 0)
            elif kind == "z2":
                if e + 1 < NE:
                    emit_tail("z1", e + 1)
                emit_tail("z2", e)
            if side is not None:
                side(pi)

    def stage_G(t0, cpar):
        gxm = [None, None]
        for tc in range(NTC):
            sl = tc % 2
            rows = x_tm[t0 + tc * 128: t0 + (tc + 1) * 128, :]
            x_dma(s_gx[:, sl, :].rearrange("p (c n) -> p c n", n=D),
                  rows.rearrange("(c p) n -> p c n", p=128),
                  wait=([gxm[sl]] if gxm[sl] else
                        [(te, te.n), (ve, ve.n), (sc, sc.n)]))
            s_op(lambda e, tc=tc: e.activation(
                out=s_cf[:, :], in_=s_attn[:, tc, :], func=ACT.Identity,
                bias=s_cols[:, cpar, 6, tc:tc + 1]),
                wait=[(ve, ve.n)])
            v_op(lambda e, sl=sl, tc=tc: e.scalar_tensor_tensor(
                out=s_jf[:, :], in0=s_gx[:, sl, :],
                scalar=s_cols[:, cpar, 7, tc:tc + 1],
                in1=s_cf[:, :], op0=ALU.mult, op1=ALU.add),
                wait=[(xd, xd.n), (ve, ve.n), (sc, sc.n)])
            gxm[sl] = (ve, ve.n)
            x_dma(out_d[t0 + tc * 128: t0 + (tc + 1) * 128, :], s_jf[:, :],
                  wait=[(ve, ve.n)])
        g_marks[0] = [(ve, ve.n), (xd, xd.n)]

    stop = cfg.stop_after
    stop_st = 0
    if stop and "@" in stop:
        stop, sst = stop.split("@")
        stop_st = int(sst)
    cpar_of = {}
    gi = 0
    for st in range(cfg.NPST):
        cpar_of[("p", st)] = gi % 2
        gi += 1
    for st in range(cfg.NST):
        cpar_of[("m", st)] = gi % 2
        gi += 1
    xnm = stage_A(xp_tm, 0, cpar_of[("p", 0)])
    pend_pDE = [None]

    def emit_pend_pDE():
        if pend_pDE[0] is not None:
            stage_DE(True, pend_pDE[0][0], epar=pend_pDE[0][1])
            pend_pDE[0] = None
    for st in range(cfg.NPST):
        cpar = cpar_of[("p", st)]
        pv = wu_next(Wv_p[:, 0:512])
        gw_ = load_B_smalls((Wgk1_p, Wgk2_p, bgk2c_p), gate=False)
        r_ = stage_B(gw_, gate=False, xnm=xnm, epar=cpar)
        eqm, emm = r_
        tpms, qbm_, kbm_ = stage_C(True, eqm, emm, pre=(pv, None), epar=cpar)
        sqw = [(ve, ve.n)]
        if st + 1 < cfg.NPST:
            nsrc, nt0, ncp = xp_tm, (st + 1) * ST, cpar_of[("p", st + 1)]
        else:
            nsrc, nt0, ncp = x_tm, 0, cpar_of[("m", 0)]
        blks = [stage_A_blk(nsrc, nt0, b, ncp, (), sq_w=sqw)
                for b in range(NBLK)]
        xnm = stage_A_xpose(blks, (te, te.n))
        if st == cfg.NPST - 1:
            g_dma(s_w4[:, :, :], _rearr(Wv[:, 0:512]),
                  wait=[(ve, ve.n), (sc, sc.n), (te, te.n)], sem=gs4)
            g_dma(s_w5[:, :, :], _rearr(Wg[:, 0:512]),
                  wait=[(tg[0], tg[0].n), (te, te.n)], sem=gs5)
            wu_pref[0] = ((s_w4, (gs4, gs4.n)), (s_w5, (gs5, gs5.n)))
        stage_DE(True, tpms, epar=cpar)
    pend_G = [None]

    def emit_pend_G():
        emit_pend_pDE()
        if pend_G[0] is not None:
            stage_G(*pend_G[0])
            pend_G[0] = None
    for st in range(cfg.NST):
        t0 = st * ST
        cpar = cpar_of[("m", st)]
        if wu_pref[0] is not None:
            dedv = wu_pref[0]
            wu_pref[0] = None
            pv = pg2 = None
        else:
            dedv = (None, None)
            pv = wu_next(Wv[:, 0:512])
            pg2 = wu_next(Wg[:, 0:512])
        gw_ = load_B_smalls((Wgk1, Wgk2, bgk2c), gate=True)
        if stop == "A" and st >= stop_st:
            break
        r_ = stage_B(gw_, gate=True, xnm=xnm,
                     stop=stop if (stop in ("Bw1", "Bw2", "Bw3", "Bs", "B0", "B1",
                                            "B2", "B3", "B4") and st >= stop_st)
                     else None, mid=emit_pend_G, epar=cpar)
        if stop in ("B", "Bw1", "Bw2", "Bw3", "Bs", "B0", "B1", "B2", "B3", "B4") \
                and st >= stop_st:
            break
        eqm, emm = r_
        tpms, qbm_, kbm_ = stage_C(False, eqm, emm, pre=(pv, pg2), ded=dedv,
                                   epar=cpar)
        if stop == "C" and st >= stop_st:
            break
        stage_DE(False, tpms, qbm_, kbm_, epar=cpar)
        if stop == "D" and st >= stop_st:
            break
        side = None
        if st + 1 < cfg.NST:
            demark = [(te, te.n), (ve, ve.n)]
            nt0, ncp = (st + 1) * ST, cpar_of[("m", st + 1)]
            nA = [None, None]
            xnm_n = [None]

            a0 = [None]

            def side(pi, demark=demark, nt0=nt0, ncp=ncp, nA=nA, xnm_n=xnm_n,
                     a0=a0):
                if pi == 1:
                    nA[0] = stage_A_blk(x_tm, nt0, 0, ncp, demark)
                    a0[0] = [(ve, ve.n), (sc, sc.n), (te, te.n)]
                elif pi == 3:
                    if NBLK > 1:
                        nA[1] = stage_A_blk(x_tm, nt0, 1, ncp, demark)
                    g_dma(s_w4[:, :, :], _rearr(Wv[:, 0:512]),
                          wait=a0[0], sem=gs4)
                elif pi == 5:
                    xnm_n[0] = stage_A_xpose(nA[:NBLK], (te, te.n))
                    g_dma(s_w5[:, :, :], _rearr(Wg[:, 0:512]),
                          wait=[(tg[0], tg[0].n), (te, te.n)], sem=gs5)
                    wu_pref[0] = ((s_w4, (gs4, gs4.n)),
                                  (s_w5, (gs5, gs5.n)))
        stage_F(side=side)
        if side is not None:
            xnm = xnm_n[0]
        if stop == "F" and st >= stop_st:
            break
        pend_G[0] = (t0, cpar)
    emit_pend_G()

    if cfg.debug:
        allw = [(te, te.n), (ve, ve.n), (sc, sc.n), (tp, tp.n), (gd, gd.n)]
        valid = {"A": ["xn", "cols"],
                 "Bs": ["xn", "cols"], "B0": ["xn", "cols"],
                 "Bw1": ["xn", "cols"], "Bw2": ["xn", "cols"],
                 "Bw3": ["xn", "cols"],
                 "Bm": ["xn", "cols"], "Bsg": ["xn", "cols"],
                 "Bln": ["xn", "cols"],
                 "B1": ["xn", "cols", "bb"],
                 "B2": ["xn", "cols", "bb", "eb"],
                 "B3": ["xn", "cols", "bb", "eb"],
                 "B4": ["xn", "cols", "bb", "eb", "w"],
                 "B": ["xn", "cols", "bb", "eb", "w"],
                 "C": ["xn", "cols", "bb", "eb", "w", "qg", "kd", "kgT", "v",
                       "silu"],
                 "D": ["xn", "cols", "eb", "w", "qg", "kd", "kgT", "v", "silu",
                       "S", "attn"],
                 "F": ["xn", "cols", "eb", "w", "qg", "kd", "kgT",
                       "S", "attn"],
                 None: ["xn", "cols", "eb", "w", "qg", "kd", "kgT",
                        "S", "attn"]}[stop]
        pairs = [("xn", s_xn[:, :, :]), ("bb", s_bb[:, :, :]),
                 ("eb", s_eb[:, 0, :, :]), ("qg", s_qg[:, :, :]),
                 ("kd", s_kd[:, :, :]), ("kgT", s_kgT[:, :, :, :]),
                 ("v", s_v[:, :, :]), ("silu", s_silu[:, :, :]),
                 ("S", s_S[:, :, :]), ("attn", s_attn[:, :, :]),
                 ("w", s_wc[:, :, :].rearrange("p a b -> p (a b)")),
                 ("cols", s_cols[:, 0, :, :])]
        first = True
        for nm, ap in pairs:
            if nm in dbg and nm in valid:
                x_dma(dbg[nm], ap, wait=allw if first else [])
                first = False

    P.emit("sync", lambda e: e.nop(),
           wait=[(xd, xd.n), (gd, gd.n), (tp, tp.n), (gx, gx.n), (gbc, gbc.n),
                 (gbr, gbr.n)] + [(s, s.n) for s in gsl] + [(s, s.n) for s in tg]
                + [(s, s.n) for s in gbr2])

    with nc.Block() as block:
        @block.sync
        def _(eng):
            for f in P.q["sync"]:
                f(eng)

        @block.scalar
        def _(eng):
            for f in P.q["scalar"]:
                f(eng)

        @block.vector
        def _(eng):
            for f in P.q["vector"]:
                f(eng)

        @block.tensor
        def _(eng):
            for f in P.q["tensor"]:
                f(eng)

        @block.gpsimd
        def _(eng):
            for f in P.q["gpsimd"]:
                f(eng)

    for cm in sem_ctx:
        cm.__exit__(None, None, None)
    return nc


def _consts():
    import ml_dtypes
    identf = np.eye(128, dtype=np.float32)
    identb = np.eye(128).astype(ml_dtypes.bfloat16)
    maskU = np.triu(np.ones((128, 128), dtype=np.float32))
    onesrow = np.ones((1, 512), dtype=ml_dtypes.bfloat16)
    onescan = np.ones((128, 128), dtype=np.float32)
    cvals = np.zeros((128, 3), np.float32)
    cvals[:, 0] = EPS
    cvals[:, 1] = LN16
    cvals[:, 2] = 1.0
    return identf, identb, maskU, onesrow, onescan, cvals


def _pk(w):
    n = w.shape[1]
    return np.ascontiguousarray(
        w.reshape(NDC, 128, n).transpose(1, 0, 2).reshape(128, NDC * n))


def make_inputs(core, cfg, inp):
    x = np.asarray(inp["L0_attn_x"])
    b, hh = core // 2, core % 2
    T = cfg.T
    xc = np.ascontiguousarray(x[b, hh * T:(hh + 1) * T, :], dtype=np.float32)
    xp = np.ascontiguousarray(x[b, 0:cfg.P, :], dtype=np.float32)
    identf, identb, maskU, onesrow, onescan, cvals = _consts()
    rms = np.asarray(inp["L0_attn_rms_w"], np.float32)
    rms_c = np.ascontiguousarray(rms.reshape(4, 128).T)
    g1 = np.asarray(inp["L0_ffn_g1"], np.float32)
    be1 = np.asarray(inp["L0_ffn_be1"], np.float32)
    g2 = np.asarray(inp["L0_ffn_g2"], np.float32)
    bgk2 = np.asarray(inp["L0_attn_bgk2"], np.float32)
    bgk2c = np.ascontiguousarray(bgk2.reshape(NKC, 128).T)

    def bc(a):
        return np.ascontiguousarray(np.broadcast_to(a[:, None, :], (NE, 128, D)))

    z = np.zeros
    im = {
        "x_tm": xc, "xp_tm": xp,
        "Wq": np.ascontiguousarray(inp["L0_attn_Wq"], np.float32),
        "Wk": np.ascontiguousarray(inp["L0_attn_Wk"], np.float32),
        "Wv": np.ascontiguousarray(inp["L0_attn_Wv"], np.float32),
        "Wg": np.ascontiguousarray(inp["L0_attn_Wg"], np.float32),
        "Wgk1": _pk(np.asarray(inp["L0_attn_Wgk1"], np.float32)),
        "Wgk2": np.ascontiguousarray(inp["L0_attn_Wgk2"], np.float32),
        "bgk2c": bgk2c,
        "Wo": np.ascontiguousarray(inp["L0_attn_Wo"], np.float32),
        "rms_c": rms_c,
        "gateW": _pk(np.asarray(inp["L0_ffn_gate_W"], np.float32)),
        "gatebr": np.asarray(inp["L0_ffn_gate_b"], np.float32).reshape(1, NE),
        "W1": np.ascontiguousarray(inp["L0_ffn_W1"], np.float32),
        "W2": np.ascontiguousarray(inp["L0_ffn_W2"], np.float32),
        "b1r": np.ascontiguousarray(inp["L0_ffn_b1"], np.float32),
        "b2r": np.ascontiguousarray(inp["L0_ffn_b2"], np.float32),
        "be2r": np.ascontiguousarray(inp["L0_ffn_be2"], np.float32),
        "g1b": bc(g1), "be1b": bc(be1), "g2b": bc(g2),
        "identf": identf, "identb": identb, "maskU": maskU,
        "onesrow": onesrow, "onescan": onescan, "cvals": cvals,
    }
    if hh == 0:
        im["Wk_p"] = z((D, DK), np.float32)
        im["Wv_p"] = z((D, DV), np.float32)
        im["Wgk1_p"] = z((128, NDC * LR), np.float32)
        im["Wgk2_p"] = z((LR, DK), np.float32)
        im["bgk2c_p"] = z((128, NKC), np.float32)
    else:
        im["Wk_p"] = im["Wk"]
        im["Wv_p"] = im["Wv"]
        im["Wgk1_p"] = im["Wgk1"]
        im["Wgk2_p"] = im["Wgk2"]
        im["bgk2c_p"] = bgk2c
    return im


def kernel(**inp):
    cfg = Cfg()
    nc = build(cfg)
    in_maps = [make_inputs(c, cfg, inp) for c in range(8)]
    res = run_bass_kernel_spmd(nc, in_maps, core_ids=list(range(8)))
    odt = np.asarray(res.results[0]["out"]).dtype
    out = np.empty((4, 4096, D), odt)
    for c in range(8):
        b, hh = c // 2, c % 2
        out[b, hh * cfg.T:(hh + 1) * cfg.T, :] = res.results[c]["out"]
    return out
